# version 01
# baseline (speedup 1.0000x reference)
import numpy as np
import ml_dtypes
from contextlib import ExitStack

import concourse.bass as bass
import concourse.mybir as mybir
from concourse.bass_utils import run_bass_kernel_spmd

F32 = mybir.dt.float32
BF16 = mybir.dt.bfloat16
AF = mybir.ActivationFunctionType
ALU = mybir.AluOpType
AX = mybir.AxisListType

B, S, D = 16, 2048, 2048
NH, HD, NS = 16, 128, 8
NE, MOE_I, SH_I = 8, 1408, 5632
NCORES = 8
SEQ_PC = 2
T = SEQ_PC * S
NC = D // 128
TTW = 512
NTT = T // TTW
NMC = MOE_I // 128
NSC = SH_I // 128
EPS_LN = 1e-5
EPS_RMS = 1e-6
ROPE_THETA = 1e6
INV_SQRT_HD = float(1.0 / np.sqrt(128.0))
NEG_BIG = -1e30

ENGINES = ("t", "v", "s", "g", "y")
ESEM = {"t": "st", "v": "sv", "s": "ss", "g": "sg", "y": "sy"}
DSEM = {"y": "dy", "g": "dg", "s": "ds"}


class Buf:
    __slots__ = ("w", "r")

    def __init__(self):
        self.w = None
        self.r = []


class Bld:
    def __init__(self, nc, stack):
        self.nc = nc
        self.ops = {e: [] for e in ENGINES}
        self.sem = {}
        self.cnt = {}
        self.lastwait = {e: {} for e in ENGINES}
        for name in ("st", "sv", "ss", "sg", "sy", "dy", "dg", "ds"):
            self.sem[name] = stack.enter_context(nc.semaphore(name))
            self.cnt[name] = 0

    def wait(self, e, tok):
        if tok is None:
            return
        sname, val = tok
        lw = self.lastwait[e]
        if lw.get(sname, 0) >= val:
            return
        lw[sname] = val
        sem = self.sem[sname]
        self.ops[e].append(lambda h, sem=sem, val=val: h.wait_ge(sem, val))

    def _deps(self, e, reads, writes):
        for b_ in reads:
            self.wait(e, b_.w)
        for b_ in writes:
            self.wait(e, b_.w)
            for t in b_.r:
                self.wait(e, t)

    def _record(self, tok, reads, writes):
        if tok is not None:
            for b_ in reads:
                b_.r.append(tok)
        for b_ in writes:
            b_.w = tok
            b_.r = []

    def do(self, e, fn, reads=(), writes=(), inc=True):
        self._deps(e, reads, writes)
        sname = ESEM[e]
        if inc:
            self.cnt[sname] += 1
            val = self.cnt[sname]
            sem = self.sem[sname]
            self.ops[e].append(lambda h, fn=fn, sem=sem: fn(h).then_inc(sem, 1))
            tok = (sname, val)
        else:
            self.ops[e].append(lambda h, fn=fn: fn(h))
            tok = None
        self._record(tok, reads, writes)
        return tok

    def mmgroup(self, fns, reads=(), writes=(), war=()):
        self._deps("t", reads, list(writes) + list(war))
        sem = self.sem["st"]
        for fn in fns[:-1]:
            self.ops["t"].append(lambda h, fn=fn: fn(h))
        self.cnt["st"] += 1
        val = self.cnt["st"]
        fn = fns[-1]
        self.ops["t"].append(lambda h, fn=fn, sem=sem: fn(h).then_inc(sem, 1))
        tok = ("st", val)
        self._record(tok, reads, writes)
        return tok

    def dma(self, qe, out, in_, reads=(), writes=(), **kw):
        return self.dma_group(qe, [(out, in_)], reads=reads, writes=writes, **kw)

    def dma_group(self, qe, items, reads=(), writes=(), **kw):
        self._deps(qe, reads, writes)
        sname = DSEM[qe]
        sem = self.sem[sname]
        for out, in_ in items:
            self.cnt[sname] += 16
            self.ops[qe].append(
                lambda h, out=out, in_=in_, sem=sem, kw=kw:
                    h.dma_start(out, in_, **kw).then_inc(sem, 16))
        tok = (sname, self.cnt[sname])
        self._record(tok, reads, writes)
        return tok

    def barrier(self):
        marks = [(sn, self.cnt[sn]) for sn in self.cnt]
        for e in ENGINES:
            for m in marks:
                self.wait(e, m)


class Carver:

    def __init__(self, handle, nelem):
        self.h = handle
        self.n = nelem
        self.off = 0
        self.hwm = 0

    def take(self, shape):
        p = shape[0]
        free = shape[1:]
        cnt = int(np.prod(free))
        assert self.off + cnt <= self.n, f"arena overflow {self.off}+{cnt} > {self.n}"
        ap = self.h[0:p, self.off:self.off + cnt]
        if len(free) > 1:
            names = "abcdef"[:len(free)]
            pat = "p (" + " ".join(names) + ") -> p " + " ".join(names)
            ap = ap.rearrange(pat, **dict(zip(names, free)))
        self.off += cnt
        self.hwm = max(self.hwm, self.off)
        return ap

    def reset(self, off=0):
        self.off = off



def _host_constants():
    bf = ml_dtypes.bfloat16
    c = {}
    c["c_ident_bf"] = np.eye(128, dtype=bf)
    c["c_ident_f32"] = np.eye(128, dtype=np.float32)
    c["c_ones_bf"] = np.ones((128, 16), dtype=bf)
    c["c_ones_f32"] = np.ones((128, 16), dtype=np.float32)
    c["c_ones_row_bf"] = np.ones((1, 128), dtype=bf)
    c["c_ones_row_f32"] = np.ones((1, 128), dtype=np.float32)
    m = np.zeros((128, 128), dtype=np.float32)
    m[np.triu_indices(128, 1)] = NEG_BIG
    c["c_mask"] = m
    inv = 1.0 / (ROPE_THETA ** (np.arange(0, HD, 2, dtype=np.float64) / HD))
    freqs = np.arange(S, dtype=np.float64)[:, None] * inv[None, :]
    emb = np.concatenate([freqs, freqs], axis=-1)
    c["c_ropecos"] = np.ascontiguousarray(np.cos(emb).T.astype(bf))
    c["c_ropesin"] = np.ascontiguousarray(np.sin(emb).T.astype(bf))
    return c


def _fm_vec(v):
    return np.ascontiguousarray(np.asarray(v, np.float32).reshape(NC, 128).T)



def build_program(debug_taps=(), stop_after=None):
    taps = set(debug_taps)
    nc = bass.Bass("TRN2", target_bir_lowering=False)

    def dram(name, shape, dtype=F32, kind="ExternalInput"):
        return nc.dram_tensor(name, shape, dtype, kind=kind).ap()

    def scratch(name, shape, dtype):
        kind = "ExternalOutput" if name in taps else "Internal"
        return nc.dram_tensor(name, shape, dtype, kind=kind).ap()

    x_in = dram("x", [T, D])
    wq_a = dram("attn_Wq", [D, D]); wk_a = dram("attn_Wk", [D, D])
    wv_a = dram("attn_Wv", [D, D]); wo_a = dram("attn_Wo", [D, D])
    sp_pos = dram("splat_pos", [128, 128])
    sp_dir = dram("splat_dir", [128, 128])
    sp_lsc = dram("splat_lsc", [128, 1])
    sp_lam = dram("splat_lam", [16, 8])
    sp_ds = dram("splat_ds", [1, 1])
    sp_gsa = dram("splat_gsa", [1, 1])
    w_ln1 = dram("w_ln1", [128, 16]); w_ln2 = dram("w_ln2", [128, 16])
    w_ln1_row = dram("w_ln1_row", [1, D])
    bq_f = dram("ffn_bq", [128, 16]); bk_f = dram("ffn_bk", [128, 16])
    bv_f = dram("ffn_bv", [1, D])
    wq_f = dram("ffn_wq", [D, D]); wk_f = dram("ffn_wk", [D, D])
    wv_f = dram("ffn_wv", [D, D]); wo_f = dram("ffn_wo", [D, D])
    w_router = dram("w_router", [D, NE])
    w_gate_e = dram("w_gate_e", [NE, D, MOE_I])
    w_up_e = dram("w_up_e", [NE, D, MOE_I])
    w_down_e = dram("w_down_e", [NE, MOE_I, D])
    w_sh_gate = dram("w_sh_gate", [D, SH_I])
    w_sh_up = dram("w_sh_up", [D, SH_I])
    w_sh_down = dram("w_sh_down", [SH_I, D])
    w_sh_eg = dram("w_sh_eg", [128, 16])
    id_bf_d = dram("c_ident_bf", [128, 128], BF16)
    id_f32_d = dram("c_ident_f32", [128, 128])
    ones_bf_d = dram("c_ones_bf", [128, 16], BF16)
    ones_f32_d = dram("c_ones_f32", [128, 16])
    ones_row_bf_d = dram("c_ones_row_bf", [1, 128], BF16)
    ones_row_f32_d = dram("c_ones_row_f32", [1, 128])
    mask_d = dram("c_mask", [128, 128])
    ropecos_d = dram("c_ropecos", [128, S], BF16)
    ropesin_d = dram("c_ropesin", [128, S], BF16)

    out = dram("out", [T, D], F32, kind="ExternalOutput")
    ln0T = scratch("ln0T", [NC, 128, T], BF16)
    qA = scratch("qA", [NC, 128, T], BF16)
    kA = scratch("kA", [NC, 128, T], BF16)
    vA = scratch("vA", [SEQ_PC, NH, S, HD], BF16)
    ctxA = scratch("ctxA", [NC, 128, T], BF16)
    x1_tok = scratch("x1_tok", [T, D], F32)
    f_inT = scratch("f_inT", [NC, 128, T], F32)
    hT = scratch("hT", [NC, 128, T], BF16)
    qF = scratch("qF", [NC, 128, T], BF16)
    kF = scratch("kF", [NC, 128, T], BF16)
    vF = scratch("vF", [SEQ_PC, NH, S, HD], BF16)
    ctxF = scratch("ctxF", [NC, 128, T], BF16)
    x1fT = scratch("x1fT", [NC, 128, T], F32)
    h2T = scratch("h2T", [NC, 128, T], BF16)
    combT = scratch("combT", [NE, T], F32)
    sig_eg_d = scratch("sig_eg", [1, T], F32)
    midT = scratch("midT", [NE, NMC, 128, T], BF16)
    smidT = scratch("smidT", [NSC, 128, T], BF16)
    y2T = scratch("y2T", [NC, 128, T], F32)
    wdn_bf = scratch("wdn_bf", [NE * NMC + NSC, 128, D], BF16)
    splat_scr = nc.dram_tensor("splat_scr", [5, 128], F32, kind="Internal").ap()

    with ExitStack() as stack:
        b = Bld(nc, stack)
        sb = lambda name, shape, dt: stack.enter_context(nc.sbuf_tensor(name, shape, dt))[:]

        ABF = Carver(sb("ABF", [128, 30720], BF16), 30720)
        ABF2 = Carver(sb("ABF2", [128, 32768], BF16), 32768)
        AF32 = Carver(sb("AF32", [128, 12288], F32), 12288)
        AF32X = Carver(sb("AF32X", [128, 5248], F32), 5248)

        P0 = stack.enter_context(nc.psum_tensor("P0", [128, 2048], F32))[:]
        P1 = stack.enter_context(nc.psum_tensor("P1", [128, 2048], F32))[:]
        PSAP = [P0[:, 512 * i:512 * (i + 1)] for i in range(4)] + \
               [P1[:, 512 * i:512 * (i + 1)] for i in range(4)]
        PSB = [Buf() for _ in range(8)]

        ident = sb("ident", [128, 128], BF16)
        identf = sb("identf", [128, 128], F32)
        onesb = sb("onesb", [128, 16], BF16)
        onesf = sb("onesf", [128, 16], F32)
        ones_row_bf = sb("ones_row_bf", [1, 128], BF16)
        ones_row_f = sb("ones_row_f", [1, 128], F32)
        maskt = sb("maskt", [128, 128], F32)
        ropec = sb("ropec", [128, S], BF16)
        ropes = sb("ropes", [128, S], BF16)
        lnw1 = sb("lnw1", [128, 16], F32)
        lnw2 = sb("lnw2", [128, 16], F32)
        bqf = sb("bqf", [128, 16], F32)
        bkf = sb("bkf", [128, 16], F32)
        wrt = sb("wrt", [128, 16, 8], F32)
        weg = sb("weg", [128, 16], BF16)
        PT = sb("PT", [128, 128], BF16)
        DT = sb("DT", [128, 128], BF16)
        possq = sb("possq", [128, 1], F32)
        pdall = sb("pdall", [128, 1], F32)
        ni2 = sb("ni2", [128, 1], F32)
        ampall = sb("ampall", [128, 1], F32)
        dsb = sb("dsb", [128, 1], F32)
        omdsb = sb("omdsb", [128, 1], F32)
        blendb = sb("blendb", [128, 1], F32)
        omblendb = sb("omblendb", [128, 1], F32)
        cbias = sb("cbias", [128, 8], F32)
        possq8 = sb("possq8", [8, 16], F32)
        pd8 = sb("pd8", [8, 16], F32)
        ni8 = sb("ni8", [8, 16], F32)
        amp8 = sb("amp8", [8, 16], F32)

        bufs = {}

        def B_(name):
            if name not in bufs:
                bufs[name] = Buf()
            return bufs[name]

        def setup_phase():
            b.do("v", lambda h: h.memset(cbias[:, 0:1], EPS_LN), writes=[B_("cbias")])
            b.do("v", lambda h: h.memset(cbias[:, 1:2], EPS_RMS), writes=[B_("cbias")])
            loads = [(ident, id_bf_d, "ident"), (identf, id_f32_d, "identf"),
                     (onesb, ones_bf_d, "onesb"), (onesf, ones_f32_d, "onesf"),
                     (ones_row_bf, ones_row_bf_d, "orb"), (ones_row_f, ones_row_f32_d, "orf"),
                     (maskt, mask_d, "maskt"), (ropec, ropecos_d, "ropec"),
                     (ropes, ropesin_d, "ropes"), (lnw1, w_ln1, "lnw1"),
                     (lnw2, w_ln2, "lnw2"), (bqf, bq_f, "bqf"), (bkf, bk_f, "bkf")]
            for dst, src, nm in loads:
                b.dma("y", dst[:], src, writes=[B_(nm)])
            b.dma_group("y", [(wrt[:, c, :], w_router[128 * c:128 * (c + 1), :]) for c in range(NC)],
                        writes=[B_("wrt")])
            b.dma("g", weg[:], w_sh_eg, writes=[B_("weg")])

            al = AF32
            al.reset()
            praw = al.take([128, 128]); draw_ = al.take([128, 128])
            tmp = al.take([128, 128]); stat = al.take([128, 8])
            b.dma("y", praw, sp_pos, writes=[B_("praw")])
            b.dma("y", draw_, sp_dir, writes=[B_("draw")])
            b.dma("y", stat[:, 0:1], sp_lsc, writes=[B_("stat")])

            def _normalize(buf_ap, bufB, extra_scale):
                b.do("s", lambda h: h.activation(tmp, buf_ap, AF.Square), reads=[bufB], writes=[B_("ptmp")])
                b.do("v", lambda h: h.reduce_sum(stat[:, 1:2], tmp, axis=AX.X), reads=[B_("ptmp")], writes=[B_("pn")])
                b.do("s", lambda h: h.activation(stat[:, 2:3], stat[:, 1:2], AF.Sqrt), reads=[B_("pn")], writes=[B_("pn2")])
                b.do("v", lambda h: h.tensor_scalar(stat[:, 2:3], stat[:, 2:3], 1e-12, None, ALU.add),
                     reads=[B_("pn2")], writes=[B_("pn2")])
                b.do("v", lambda h: h.reciprocal(stat[:, 3:4], stat[:, 2:3]), reads=[B_("pn2")], writes=[B_("pn3")])
                if extra_scale is None:
                    b.do("v", lambda h: h.tensor_scalar(buf_ap, buf_ap, stat[:, 3:4], None, ALU.mult),
                         reads=[B_("pn3")], writes=[bufB])
                else:
                    b.do("v", lambda h: h.tensor_scalar(buf_ap, buf_ap, stat[:, 3:4], extra_scale,
                                                        ALU.mult, ALU.mult),
                         reads=[B_("pn3")], writes=[bufB])

            _normalize(praw, B_("praw"), float(np.sqrt(HD) * 0.3))
            _normalize(draw_, B_("draw"), None)
            b.do("s", lambda h: h.activation(tmp, praw, AF.Square), reads=[B_("praw")], writes=[B_("ptmp")])
            b.do("v", lambda h: h.reduce_sum(possq[:], tmp, axis=AX.X), reads=[B_("ptmp")], writes=[B_("possq")])
            b.do("v", lambda h: h.tensor_mul(tmp, praw, draw_), reads=[B_("praw"), B_("draw")], writes=[B_("ptmp")])
            b.do("v", lambda h: h.reduce_sum(pdall[:], tmp, axis=AX.X), reads=[B_("ptmp")], writes=[B_("pdall")])
            b.do("s", lambda h: h.activation(stat[:, 4:5], stat[:, 0:1], AF.Exp), reads=[B_("stat")], writes=[B_("sc1")])
            b.do("v", lambda h: h.tensor_scalar(stat[:, 4:5], stat[:, 4:5], 0.3, 1.2, ALU.max, ALU.min),
                 reads=[B_("sc1")], writes=[B_("sc1")])
            b.do("s", lambda h: h.activation(stat[:, 5:6], stat[:, 4:5], AF.Square), reads=[B_("sc1")], writes=[B_("sc2")])
            b.do("v", lambda h: h.reciprocal(stat[:, 6:7], stat[:, 5:6]), reads=[B_("sc2")], writes=[B_("sc3")])
            b.do("v", lambda h: h.tensor_scalar(ni2[:], stat[:, 6:7], -0.5, None, ALU.mult),
                 reads=[B_("sc3")], writes=[B_("ni2")])
            amp16 = al.take([16, 8]); amps = al.take([16, 4])
            b.dma("y", amp16, sp_lam, writes=[B_("amp16")])
            b.do("v", lambda h: h.reduce_max(amps[:, 0:1], amp16, axis=AX.X, negate=True),
                 reads=[B_("amp16")], writes=[B_("ampm")])
            b.do("s", lambda h: h.activation(amp16, amp16, AF.Exp, bias=amps[:, 0:1], accum_out=amps[:, 1:2]),
                 reads=[B_("amp16"), B_("ampm")], writes=[B_("amp16"), B_("ampsum")])
            b.do("v", lambda h: h.reciprocal(amps[:, 2:3], amps[:, 1:2]), reads=[B_("ampsum")], writes=[B_("amprec")])
            b.do("v", lambda h: h.tensor_scalar(amp16, amp16, amps[:, 2:3], None, ALU.mult),
                 reads=[B_("amp16"), B_("amprec")], writes=[B_("amp16")])
            b.dma("y", splat_scr[4:5, :].rearrange("a (h n) -> (a h) n", n=8), amp16,
                  reads=[B_("amp16")], writes=[B_("sd4")])
            b.dma("y", ampall[:], splat_scr[4:5, :].rearrange("a (b c) -> (a b) c", c=1),
                  reads=[B_("sd4")], writes=[B_("ampall")])
            sds = al.take([1, 4])
            b.dma("y", sds[0:1, 0:1], sp_ds, writes=[B_("sds")])
            b.dma("y", sds[0:1, 1:2], sp_gsa, writes=[B_("sds")])
            b.do("s", lambda h: h.activation(sds[0:1, 0:2], sds[0:1, 0:2], AF.Sigmoid),
                 reads=[B_("sds")], writes=[B_("sds")])
            b.do("v", lambda h: h.tensor_scalar(sds[0:1, 1:2], sds[0:1, 1:2], 0.1, 0.05, ALU.mult, ALU.min),
                 reads=[B_("sds")], writes=[B_("sds")])
            b.mmgroup([lambda h: h.matmul(P0[:, 0:2], ones_row_f[0:1, :], sds[0:1, 0:2],
                                          start=True, stop=True)],
                      reads=[B_("sds"), B_("orf")], writes=[PSB[0]])
            b.do("v", lambda h: h.tensor_copy(dsb[:], P0[:, 0:1]), reads=[PSB[0]], writes=[B_("dsb")])
            b.do("v", lambda h: h.tensor_copy(blendb[:], P0[:, 1:2]), reads=[PSB[0]], writes=[B_("blendb")])
            b.do("v", lambda h: h.tensor_scalar(omdsb[:], dsb[:], -1.0, 1.0, ALU.mult, ALU.add),
                 reads=[B_("dsb")], writes=[B_("omdsb")])
            b.do("v", lambda h: h.tensor_scalar(omblendb[:], blendb[:], -1.0, 1.0, ALU.mult, ALU.add),
                 reads=[B_("blendb")], writes=[B_("omblendb")])
            b.mmgroup([lambda h: h.matmul(P1[:, 0:128], praw, identf, start=True, stop=True)],
                      reads=[B_("praw"), B_("identf")], writes=[PSB[4]])
            b.do("v", lambda h: h.tensor_copy(PT[:], P1[:, 0:128]), reads=[PSB[4]], writes=[B_("PT")])
            b.mmgroup([lambda h: h.matmul(P1[:, 512:640], draw_, identf, start=True, stop=True)],
                      reads=[B_("draw"), B_("identf")], writes=[PSB[5]])
            b.do("v", lambda h: h.tensor_copy(DT[:], P1[:, 512:640]), reads=[PSB[5]], writes=[B_("DT")])
            for i, (src, dst, bn) in enumerate(((possq, possq8, "possq"), (pdall, pd8, "pdall"),
                                                (ni2, ni8, "ni2"), (ampall, amp8, "ampall"))):
                b.dma("y", splat_scr[i:i + 1, :].rearrange("a (b c) -> (a b) c", c=1), src[:],
                      reads=[B_(bn)], writes=[B_("sd%d" % i)])
                b.dma_group("y", [(dst[:, hh:hh + 1],
                                   splat_scr[i:i + 1, 8 * hh:8 * (hh + 1)].rearrange("a (b c) -> (a b) c", c=1))
                                  for hh in range(NH)],
                            reads=[B_("sd%d" % i)], writes=[B_(bn + "8")])

        def phase_ln0():
            al = AF32
            al.reset()
            xt = [al.take([128, 2048]) for _ in range(2)]
            sq = al.take([128, 2048])
            st_ = al.take([128, 8])
            xtB = [B_("p0x%d" % i) for i in range(2)]
            sqB = B_("p0sq"); stB = B_("p0st")
            ABF.reset()
            lnb = [ABF.take([128, 2048]) for _ in range(4)]
            lnbB = [B_("p0l%d" % i) for i in range(4)]
            evb = [ABF.take([128, 512]) for _ in range(4)]
            evbB = [B_("p0e%d" % i) for i in range(4)]
            for g in range(NTT):
                for j in range(4):
                    tb = 4 * g + j
                    xb = xt[tb % 2]; xbB = xtB[tb % 2]
                    b.dma("y", xb, x_in[128 * tb:128 * (tb + 1), :], writes=[xbB])
                    b.do("v", lambda h, xb=xb: h.reduce_sum(st_[:, 0:1], xb, axis=AX.X),
                         reads=[xbB], writes=[stB])
                    b.do("s", lambda h, xb=xb: h.activation(sq, xb, AF.Square),
                         reads=[xbB], writes=[sqB])
                    b.do("v", lambda h: h.reduce_sum(st_[:, 1:2], sq, axis=AX.X),
                         reads=[sqB], writes=[stB])
                    b.do("v", lambda h: h.tensor_scalar(st_[:, 2:3], st_[:, 0:1], 1.0 / D, None, ALU.mult),
                         reads=[stB], writes=[stB])
                    b.do("v", lambda h: h.tensor_mul(st_[:, 3:4], st_[:, 2:3], st_[:, 2:3]),
                         reads=[stB], writes=[stB])
                    b.do("v", lambda h: h.scalar_tensor_tensor(st_[:, 4:5], st_[:, 1:2], 1.0 / D,
                                                               st_[:, 3:4], ALU.mult, ALU.subtract),
                         reads=[stB], writes=[stB])
                    b.do("s", lambda h: h.activation(st_[:, 5:6], st_[:, 4:5], AF.Sqrt, bias=cbias[:, 0:1]),
                         reads=[stB, B_("cbias")], writes=[stB])
                    b.do("v", lambda h: h.reciprocal(st_[:, 6:7], st_[:, 5:6]),
                         reads=[stB], writes=[stB])
                    b.do("v", lambda h, xb=xb, o=lnb[j]: h.tensor_scalar(
                            o, xb, st_[:, 2:3], st_[:, 6:7], ALU.subtract, ALU.mult),
                         reads=[xbB, stB], writes=[lnbB[j]])
                for c in range(NC):
                    bank = c % 4
                    fns = [(lambda h, j=j, c=c, bank=bank:
                            h.matmul(PSAP[bank][:, 128 * j:128 * (j + 1)],
                                     lnb[j][:, 128 * c:128 * (c + 1)], ident,
                                     start=True, stop=True)) for j in range(4)]
                    b.mmgroup(fns, reads=[lnbB[0], lnbB[1], lnbB[2], lnbB[3], B_("ident")],
                              writes=[PSB[bank]])
                    eb = evb[c % 4]; ebB = evbB[c % 4]
                    if c % 2 == 0:
                        b.do("v", lambda h, eb=eb, bank=bank: h.tensor_copy(eb, PSAP[bank]),
                             reads=[PSB[bank]], writes=[ebB])
                    else:
                        b.do("s", lambda h, eb=eb, bank=bank: h.activation(eb, PSAP[bank], AF.Copy),
                             reads=[PSB[bank]], writes=[ebB])
                    b.dma("s", ln0T[c, :, 512 * g:512 * (g + 1)], eb, reads=[ebB])
            b.barrier()

        def wload(wbuf, wbufB, rows_of):
            b.dma_group("g", [(wbuf[:, k, :], rows_of(k)) for k in range(NC)],
                        writes=[wbufB])

        def gemm_fm(src_scr, wbuf, wbufB, Mn, evac_cb, xb2, xb2B, width=TTW, xq="s"):
            ntt = T // width
            for tt in range(ntt):
                xb = xb2[tt % 2]; xbB = xb2B[tt % 2]
                b.dma_group(xq, [(xb[:, k, :], src_scr[k, :, width * tt:width * (tt + 1)])
                                 for k in range(NC)], writes=[xbB])
                for m in range(Mn):
                    bank = m % 4
                    fns = [(lambda h, k=k, m=m, bank=bank, xb=xb:
                            h.matmul(PSAP[bank][:, 0:width], wbuf[:, k, 128 * m:128 * (m + 1)],
                                     xb[:, k, :], start=(k == 0), stop=(k == NC - 1)))
                           for k in range(NC)]
                    b.mmgroup(fns, reads=[wbufB, xbB], writes=[PSB[bank]])
                    evac_cb(tt, m, bank)

        def mk_evac_bf16(dst_scr, ev4, ev4B, width=TTW, qdma="y"):
            def cb(tt, m, bank):
                eb = ev4[m % 4]; ebB = ev4B[m % 4]
                if m % 2 == 0:
                    b.do("v", lambda h, eb=eb, bank=bank: h.tensor_copy(eb, PSAP[bank][:, 0:width]),
                         reads=[PSB[bank]], writes=[ebB])
                else:
                    b.do("s", lambda h, eb=eb, bank=bank: h.activation(eb, PSAP[bank][:, 0:width], AF.Copy),
                         reads=[PSB[bank]], writes=[ebB])
                b.dma(qdma, dst_scr[m, :, width * tt:width * (tt + 1)], eb, reads=[ebB])
            return cb

        def phase_qkv(block):
            a = (block == "A")
            src = ln0T if a else hT
            wq_, wk_, wv_ = (wq_a, wk_a, wv_a) if a else (wq_f, wk_f, wv_f)
            qdst, kdst, vdst = (qA, kA, vA) if a else (qF, kF, vF)
            ABF.reset()
            xb2 = [ABF.take([128, NC, TTW]) for _ in range(2)]
            xb2B = [B_("qkvx%d" % i) for i in range(2)]
            ev4 = [ABF.take([128, TTW]) for _ in range(4)]
            ev4B = [B_("qkve%d" % i) for i in range(4)]
            ABF2.reset()
            wbuf = ABF2.take([128, NC, D])
            wbufB = B_("qkvw")
            AF32.reset()
            evf = [AF32.take([128, TTW]) for _ in range(2)]
            evfB = [B_("qkvf%d" % i) for i in range(2)]
            rot = AF32.take([128, TTW])
            rotB = B_("qkvrot")
            bvb = AF32.take([128, D]) if not a else None
            bvrow = AF32.take([1, D]) if not a else None

            if not a:
                b.dma("y", bvrow, bv_f, writes=[B_("bvrow")])
                for n in range(4):
                    b.mmgroup([lambda h, n=n: h.matmul(P1[:, 512 * n:512 * (n + 1)],
                                                       ones_row_f[0:1, :], bvrow[0:1, 512 * n:512 * (n + 1)],
                                                       start=True, stop=True)],
                              reads=[B_("bvrow"), B_("orf")], writes=[PSB[4 + n]])
                    b.do("v", lambda h, n=n: h.tensor_copy(bvb[:, 512 * n:512 * (n + 1)],
                                                           P1[:, 512 * n:512 * (n + 1)]),
                         reads=[PSB[4 + n]], writes=[B_("bvb")])

            for which, wsrc, dst, bias_ap, biasB in (("q", wq_, qdst, (None if a else bqf), "bqf"),
                                                     ("k", wk_, kdst, (None if a else bkf), "bkf")):
                wload(wbuf, wbufB, lambda k, wsrc=wsrc: wsrc[128 * k:128 * (k + 1), :])

                if a:
                    cb = mk_evac_bf16(dst, ev4, ev4B)
                else:
                    def cb(tt, m, bank, dst=dst, bias_ap=bias_ap, biasB=biasB):
                        pos0 = (tt % 4) * TTW
                        ef = evf[m % 2]; efB = evfB[m % 2]
                        eb = ev4[m % 4]; ebB = ev4B[m % 4]
                        b.do("s", lambda h, ef=ef, bank=bank, m=m:
                             h.activation(ef, PSAP[bank], AF.Identity, bias=bias_ap[:, m:m + 1]),
                             reads=[PSB[bank], B_(biasB)], writes=[efB])
                        b.do("v", lambda h, ef=ef: h.tensor_scalar(rot[0:64, :], ef[64:128, :],
                                                                   -1.0, None, ALU.mult),
                             reads=[efB], writes=[rotB])
                        b.do("v", lambda h, ef=ef: h.tensor_copy(rot[64:128, :], ef[0:64, :]),
                             reads=[efB], writes=[rotB])
                        b.do("v", lambda h, ef=ef, pos0=pos0:
                             h.tensor_mul(ef, ef, ropec[:, pos0:pos0 + TTW]),
                             reads=[efB, B_("ropec")], writes=[efB])
                        b.do("v", lambda h, pos0=pos0:
                             h.tensor_mul(rot, rot, ropes[:, pos0:pos0 + TTW]),
                             reads=[rotB, B_("ropes")], writes=[rotB])
                        b.do("v", lambda h, ef=ef, eb=eb: h.tensor_add(eb, ef, rot),
                             reads=[efB, rotB], writes=[ebB])
                        b.dma("y", dst[m, :, TTW * tt:TTW * (tt + 1)], eb, reads=[ebB])
                gemm_fm(src, wbuf, wbufB, NC, cb, xb2, xb2B, xq="s")

            wload(wbuf, wbufB, lambda k: wv_[128 * k:128 * (k + 1), :])
            for tt in range(NTT):
                xb = xb2[tt % 2]; xbB = xb2B[tt % 2]
                b.dma_group("s", [(xb[:, k, :], src[k, :, TTW * tt:TTW * (tt + 1)])
                                  for k in range(NC)], writes=[xbB])
                sidx = tt // 4
                for tb in range(4):
                    t0 = (tt % 4) * TTW + tb * 128
                    for n in range(4):
                        bank = (tb * 4 + n) % 4
                        fns = [(lambda h, k=k, tb=tb, n=n, bank=bank, xb=xb:
                                h.matmul(PSAP[bank], xb[:, k, 128 * tb:128 * (tb + 1)],
                                         wbuf[:, k, 512 * n:512 * (n + 1)],
                                         start=(k == 0), stop=(k == NC - 1)))
                               for k in range(NC)]
                        b.mmgroup(fns, reads=[wbufB, xbB], writes=[PSB[bank]])
                        eb = ev4[bank]; ebB = ev4B[bank]
                        if a:
                            b.do("v", lambda h, eb=eb, bank=bank: h.tensor_copy(eb, PSAP[bank]),
                                 reads=[PSB[bank]], writes=[ebB])
                        else:
                            b.do("v", lambda h, eb=eb, bank=bank, n=n:
                                 h.tensor_add(eb, PSAP[bank], bvb[:, 512 * n:512 * (n + 1)]),
                                 reads=[PSB[bank], B_("bvb")], writes=[ebB])
                        b.dma_group("y", [(vdst[sidx, 4 * n + j, t0:t0 + 128, :],
                                           eb[:, 128 * j:128 * (j + 1)]) for j in range(4)],
                                    reads=[ebB])
            b.barrier()

        def phase_attn(block):
            a = (block == "A")
            qsrc, ksrc, vsrc, cdst = (qA, kA, vA, ctxA) if a else (qF, kF, vF, ctxF)
            ABF.reset()
            qh2 = [ABF.take([128, S]) for _ in range(2)]
            kh2 = [ABF.take([128, S]) for _ in range(2)]
            vh2 = [ABF.take([128, NC, 129]) for _ in range(2)]
            ats2 = [ABF.take([128, S]) for _ in range(2)]
            ctxb = ABF.take([128, S])
            csb = ABF.take([128, S])
            qsq = ABF.take([128, S])
            qa2 = ABF.take([8, S])
            ka2 = ABF.take([8, S])
            katok = ABF.take([128, NC, 8])
            kvb = ABF.take([8, 129])
            qh2B = [B_("ath_q%d" % i) for i in range(2)]
            kh2B = [B_("ath_k%d" % i) for i in range(2)]
            vh2B = [B_("ath_v%d" % i) for i in range(2)]
            ats2B = [B_("ath_t%d" % i) for i in range(2)]
            ctxbB = B_("ath_ctx"); csbB = B_("ath_csb"); qsqB = B_("ath_qsq")
            qa2B = B_("ath_qa2"); ka2B = B_("ath_ka2"); katokB = B_("ath_katok")
            kvbB = B_("ath_kvb")
            ABF2.reset()
            attn = ABF2.take([128, NC, S])
            attnB = B_("ath_attn")
            AF32.reset()
            SA = AF32.take([8, S]); SB2 = AF32.take([8, S])
            SC2 = AF32.take([8, S]); SD = AF32.take([8, S])
            tqs = AF32.take([1, S])
            SAB = B_("ath_SA"); SB2B = B_("ath_SB2"); SC2B = B_("ath_SC2")
            SDB = B_("ath_SD"); tqsB = B_("ath_tqs")
            AF32X.reset()
            stt_ = AF32X.take([128, 64])
            sttB = B_("ath_stt")
            nmx = stt_[:, 0:16]; nms = stt_[:, 16:32]; rs_ = stt_[:, 32:48]; rr_ = stt_[:, 48:64]

            if a:
                for i in range(2):
                    b.do("v", lambda h, i=i: h.memset(vh2[i][:, :, 128:129], 1.0), writes=[vh2B[i]])

            def load_head(s, hh, par):
                b.dma("y", qh2[par], qsrc[hh, :, S * s:S * (s + 1)], writes=[qh2B[par]])
                b.dma("s", kh2[par], ksrc[hh, :, S * s:S * (s + 1)], writes=[kh2B[par]])
                b.dma_group("g", [(vh2[par][:, c, 0:128], vsrc[s, hh, 128 * c:128 * (c + 1), :])
                                  for c in range(NC)], writes=[vh2B[par]])

            load_head(0, 0, 0)
            for sh in range(SEQ_PC * NH):
                s, hh = sh // NH, sh % NH
                par = sh % 2
                if sh + 1 < SEQ_PC * NH:
                    load_head((sh + 1) // NH, (sh + 1) % NH, 1 - par)
                qh, qhB = qh2[par], qh2B[par]
                kh, khB = kh2[par], kh2B[par]
                vh, vhB = vh2[par], vh2B[par]

                for qt in range(NC):
                    nk = (qt + 1) * 128 if not a else S
                    pg = [PSB[0], PSB[1], PSB[2], PSB[3]] if qt % 2 == 0 else \
                         [PSB[4], PSB[5], PSB[6], PSB[7]]
                    P_ = P0 if qt % 2 == 0 else P1
                    nb = (nk + 511) // 512
                    fns = []
                    for j in range(nb):
                        w = min(512, nk - 512 * j)
                        fns.append(lambda h, j=j, w=w, qt=qt, P_=P_, qh=qh, kh=kh:
                                   h.matmul(P_[:, 512 * j:512 * j + w],
                                            qh[:, 128 * qt:128 * (qt + 1)],
                                            kh[:, 512 * j:512 * j + w],
                                            start=True, stop=True))
                    b.mmgroup(fns, reads=[qhB, khB], writes=pg[:nb])
                    if not a:
                        b.do("v", lambda h, P_=P_, qt=qt: h.tensor_tensor(
                                P_[:, 128 * qt:128 * (qt + 1)], P_[:, 128 * qt:128 * (qt + 1)],
                                maskt, ALU.add),
                             reads=[B_("maskt")] + pg[qt // 4:qt // 4 + 1], writes=[pg[qt // 4]])
                    b.do("v", lambda h, P_=P_, nk=nk, qt=qt: h.reduce_max(
                            nmx[:, qt:qt + 1], P_[:, 0:nk], axis=AX.X, negate=True),
                         reads=pg[:nb], writes=[sttB])
                    b.do("v", lambda h, qt=qt: h.tensor_scalar(
                            nms[:, qt:qt + 1], nmx[:, qt:qt + 1], INV_SQRT_HD, None, ALU.mult),
                         reads=[sttB], writes=[sttB])
                    b.do("s", lambda h, P_=P_, nk=nk, qt=qt: h.activation(
                            attn[:, qt, 0:nk], P_[:, 0:nk], AF.Exp, scale=INV_SQRT_HD,
                            bias=nms[:, qt:qt + 1], accum_out=rs_[:, qt:qt + 1]),
                         reads=pg[:nb] + [sttB], writes=[attnB, sttB])
                    b.do("v", lambda h, qt=qt: h.reciprocal(rr_[:, qt:qt + 1], rs_[:, qt:qt + 1]),
                         reads=[sttB], writes=[sttB])
                    b.do("v", lambda h, nk=nk, qt=qt: h.tensor_scalar(
                            attn[:, qt, 0:nk], attn[:, qt, 0:nk], rr_[:, qt:qt + 1], None, ALU.mult),
                         reads=[sttB, attnB], writes=[attnB])

                if a:
                    def affin(xh, xhB, out_bf, out_bfB, hh=hh):
                        b.do("s", lambda h, xh=xh: h.activation(qsq, xh, AF.Square),
                             reads=[xhB], writes=[qsqB])
                        fns = [(lambda h, j=j, hh=hh, xh=xh: h.matmul(P0[0:8, 512 * j:512 * (j + 1)],
                                                        PT[:, 8 * hh:8 * hh + 8], xh[:, 512 * j:512 * (j + 1)],
                                                        start=True, stop=True)) for j in range(4)]
                        b.mmgroup(fns, reads=[xhB, B_("PT")], writes=PSB[0:4])
                        fns = [(lambda h, j=j, hh=hh, xh=xh: h.matmul(P1[0:8, 512 * j:512 * (j + 1)],
                                                        DT[:, 8 * hh:8 * hh + 8], xh[:, 512 * j:512 * (j + 1)],
                                                        start=True, stop=True)) for j in range(4)]
                        b.mmgroup(fns, reads=[xhB, B_("DT")], writes=PSB[4:8])
                        b.do("v", lambda h: h.tensor_copy(SA, P0[0:8, :]), reads=PSB[0:4], writes=[SAB])
                        b.do("v", lambda h: h.tensor_copy(SB2, P1[0:8, :]), reads=PSB[4:8], writes=[SB2B])
                        fns = [(lambda h, j=j: h.matmul(P0[0:1, 512 * j:512 * (j + 1)],
                                                        onesb[:, 0:1], qsq[:, 512 * j:512 * (j + 1)],
                                                        start=True, stop=True)) for j in range(4)]
                        b.mmgroup(fns, reads=[qsqB, B_("onesb")], writes=PSB[0:4])
                        b.do("v", lambda h: h.tensor_copy(tqs, P0[0:1, :]), reads=PSB[0:4], writes=[tqsB])
                        fns = [(lambda h, j=j: h.matmul(P1[0:8, 512 * j:512 * (j + 1)],
                                                        ones_row_f[0:1, 0:8], tqs[0:1, 512 * j:512 * (j + 1)],
                                                        start=True, stop=True)) for j in range(4)]
                        b.mmgroup(fns, reads=[tqsB, B_("orf")], writes=PSB[4:8])
                        b.do("v", lambda h: h.scalar_tensor_tensor(SA, SA, -2.0, P1[0:8, :],
                                                                   ALU.mult, ALU.add),
                             reads=[SAB] + PSB[4:8], writes=[SAB])
                        b.do("v", lambda h, hh=hh: h.tensor_scalar(SA, SA, possq8[:, hh:hh + 1], 0.0,
                                                            ALU.add, ALU.max),
                             reads=[SAB, B_("possq8")], writes=[SAB])
                        b.do("s", lambda h, hh=hh: h.activation(SC2, SA, AF.Exp, scale=ni8[:, hh:hh + 1]),
                             reads=[SAB, B_("ni28")], writes=[SC2B])
                        b.do("v", lambda h, hh=hh: h.tensor_scalar(SB2, SB2, pd8[:, hh:hh + 1], None, ALU.subtract),
                             reads=[SB2B, B_("pdall8")], writes=[SB2B])
                        b.do("v", lambda h: h.tensor_mul(SD, SB2, SB2), reads=[SB2B], writes=[SDB])
                        b.do("v", lambda h: h.tensor_sub(SD, SA, SD), reads=[SAB, SDB], writes=[SDB])
                        b.do("v", lambda h: h.tensor_scalar(SD, SD, 0.0, None, ALU.max),
                             reads=[SDB], writes=[SDB])
                        b.do("s", lambda h, hh=hh: h.activation(SB2, SD, AF.Exp, scale=ni8[:, hh:hh + 1]),
                             reads=[SDB, B_("ni28")], writes=[SB2B])
                        b.do("v", lambda h: h.tensor_scalar(SB2, SB2, dsb[0:8, 0:1], None, ALU.mult),
                             reads=[SB2B, B_("dsb")], writes=[SB2B])
                        b.do("v", lambda h: h.scalar_tensor_tensor(SC2, SC2, omdsb[0:8, 0:1], SB2,
                                                                   ALU.mult, ALU.add),
                             reads=[SC2B, SB2B, B_("omdsb")], writes=[SC2B])
                        b.do("v", lambda h, hh=hh, out_bf=out_bf: h.tensor_scalar(out_bf, SC2, amp8[:, hh:hh + 1], None, ALU.mult),
                             reads=[SC2B, B_("ampall8")], writes=[out_bfB])
                    affin(qh, qhB, qa2, qa2B)
                    affin(kh, khB, ka2, ka2B)
                    fns = [(lambda h, c=c: h.matmul(P0[:, 1536 + 8 * c:1544 + 8 * c],
                                                    ka2[:, 128 * c:128 * (c + 1)], ident[0:8, 0:8],
                                                    start=True, stop=True)) for c in range(NC)]
                    b.mmgroup(fns, reads=[ka2B, B_("ident")], writes=[PSB[3]])
                    b.do("v", lambda h: h.tensor_copy(katok.rearrange("p a b -> p (a b)"),
                                                      P0[:, 1536:1664]),
                         reads=[PSB[3]], writes=[katokB])
                    fns = [(lambda h, c=c, vh=vh: h.matmul(P1[0:8, 0:129], katok[:, c, :], vh[:, c, :],
                                                           start=(c == 0), stop=(c == NC - 1)))
                           for c in range(NC)]
                    b.mmgroup(fns, reads=[katokB, vhB], writes=[PSB[4]])
                    b.do("v", lambda h: h.tensor_copy(kvb, P1[0:8, 0:129]), reads=[PSB[4]], writes=[kvbB])
                    fns = [(lambda h, j=j: h.matmul(P0[0:1, 512 * j:512 * (j + 1)],
                                                    kvb[:, 128:129], qa2[:, 512 * j:512 * (j + 1)],
                                                    start=True, stop=True)) for j in range(4)]
                    b.mmgroup(fns, reads=[kvbB, qa2B], writes=PSB[0:4])
                    b.do("v", lambda h: h.tensor_copy(tqs, P0[0:1, :]), reads=PSB[0:4], writes=[tqsB])
                    b.do("v", lambda h: h.tensor_scalar(tqs, tqs, 1e-8, None, ALU.add),
                         reads=[tqsB], writes=[tqsB])
                    b.do("v", lambda h: h.reciprocal(tqs, tqs), reads=[tqsB], writes=[tqsB])
                    fns = [(lambda h, j=j: h.matmul(P1[0:8, 512 * j:512 * (j + 1)],
                                                    ones_row_f[0:1, 0:8], tqs[0:1, 512 * j:512 * (j + 1)],
                                                    start=True, stop=True)) for j in range(4)]
                    b.mmgroup(fns, reads=[tqsB, B_("orf")], writes=PSB[4:8])
                    b.do("v", lambda h: h.tensor_tensor(qa2, qa2, P1[0:8, :], ALU.mult),
                         reads=[qa2B] + PSB[4:8], writes=[qa2B])

                for kc in range(NC):
                    qt_lo = 0 if a else kc
                    for p in range(qt_lo // 4, 4):
                        bank = p % 2
                        qts = [q_ for q_ in range(max(qt_lo, 4 * p), 4 * p + 4)]
                        fns = [(lambda h, q_=q_, p=p, bank=bank, kc=kc:
                                h.matmul(P0[:, 512 * bank + 128 * (q_ - 4 * p):512 * bank + 128 * (q_ - 4 * p) + 128],
                                         attn[:, q_, 128 * kc:128 * (kc + 1)], ident,
                                         start=True, stop=True)) for q_ in qts]
                        b.mmgroup(fns, reads=[attnB, B_("ident")], writes=[PSB[bank]])
                        ats = ats2[kc % 2]; atsB = ats2B[kc % 2]
                        q0 = max(qt_lo, 4 * p) * 128
                        w = (4 * p + 4) * 128 - q0
                        eng = "v" if p % 2 == 0 else "s"
                        off = 512 * bank + (q0 - 512 * p)
                        if eng == "v":
                            b.do("v", lambda h, ats=ats, off=off, w=w, q0=q0:
                                 h.tensor_copy(ats[:, q0:q0 + w], P0[:, off:off + w]),
                                 reads=[PSB[bank]], writes=[atsB])
                        else:
                            b.do("s", lambda h, ats=ats, off=off, w=w, q0=q0:
                                 h.activation(ats[:, q0:q0 + w], P0[:, off:off + w], AF.Copy),
                                 reads=[PSB[bank]], writes=[atsB])
                    ats = ats2[kc % 2]; atsB = ats2B[kc % 2]
                    q0all = qt_lo * 128
                    first_j = q0all // 512
                    fns = []
                    for j in range(first_j, 4):
                        lo = max(q0all, 512 * j)
                        wdt = 512 * (j + 1) - lo
                        fns.append(lambda h, lo=lo, wdt=wdt, kc=kc, vh=vh, ats=ats:
                                   h.matmul(P1[:, lo:lo + wdt], vh[:, kc, 0:128], ats[:, lo:lo + wdt],
                                            start=(kc == 0), stop=(kc == NC - 1)))
                    if kc == 0:
                        b.mmgroup(fns, reads=[atsB, vhB], war=PSB[4:8], writes=[])
                    elif kc == NC - 1:
                        b.mmgroup(fns, reads=[atsB, vhB], writes=PSB[4:8])
                    else:
                        b.mmgroup(fns, reads=[atsB, vhB])

                if a:
                    for j in range(4):
                        b.do("v", lambda h, j=j: h.tensor_scalar(
                                csb[:, 512 * j:512 * (j + 1)], P1[:, 512 * j:512 * (j + 1)],
                                omblendb[:, 0:1], None, ALU.mult),
                             reads=[PSB[4 + j], B_("omblendb")], writes=[csbB])
                    fns = [(lambda h, j=j: h.matmul(P0[:, 512 * j:512 * (j + 1)],
                                                    kvb[:, 0:128], qa2[:, 512 * j:512 * (j + 1)],
                                                    start=True, stop=True)) for j in range(4)]
                    b.mmgroup(fns, reads=[kvbB, qa2B], writes=PSB[0:4])
                    for j in range(4):
                        b.do("v", lambda h, j=j: h.scalar_tensor_tensor(
                                ctxb[:, 512 * j:512 * (j + 1)], P0[:, 512 * j:512 * (j + 1)],
                                blendb[:, 0:1], csb[:, 512 * j:512 * (j + 1)], ALU.mult, ALU.add),
                             reads=[PSB[j], csbB, B_("blendb")], writes=[ctxbB])
                else:
                    for j in range(4):
                        eng = "v" if j % 2 == 0 else "s"
                        if eng == "v":
                            b.do("v", lambda h, j=j: h.tensor_copy(
                                    ctxb[:, 512 * j:512 * (j + 1)], P1[:, 512 * j:512 * (j + 1)]),
                                 reads=[PSB[4 + j]], writes=[ctxbB])
                        else:
                            b.do("s", lambda h, j=j: h.activation(
                                    ctxb[:, 512 * j:512 * (j + 1)], P1[:, 512 * j:512 * (j + 1)], AF.Copy),
                                 reads=[PSB[4 + j]], writes=[ctxbB])
                b.dma("y", cdst[hh, :, S * s:S * (s + 1)], ctxb, reads=[ctxbB])
            b.barrier()

        def phase_p3():
            ABF.reset()
            xb2 = [ABF.take([128, NC, TTW]) for _ in range(2)]
            xb2B = [B_("p3x%d" % i) for i in range(2)]
            ABF2.reset()
            wbuf = ABF2.take([128, NC, D]); wbufB = B_("p3w")
            AF32.reset()
            EB = AF32.take([128, NC, TTW])
            EBB = B_("p3eb")
            xtok = [AF32.take([128, D]) for _ in range(1)]
            xtokB = [B_("p3xt0")]
            aot = AF32.take([128, D]); aotB = B_("p3aot")
            hbf = ABF.take([128, D]); hbfB = B_("p3hbf")
            hst = ABF.take([128, D]); hstB = B_("p3hst")
            AF32X.reset()
            st2 = AF32X.take([128, 16]); st2B = B_("p3st")
            f32st = [AF32X.take([128, TTW]) for _ in range(2)]
            f32stB = [B_("p3f%d" % i) for i in range(2)]
            W1B = AF32X.take([128, D]); W1BB = B_("p3w1b")
            w1row = AF32X.take([1, D]); w1rowB = B_("p3w1r")

            b.dma("y", w1row, w_ln1_row, writes=[w1rowB])
            for n in range(4):
                b.mmgroup([(lambda h, n=n: h.matmul(P1[:, 512 * n:512 * (n + 1)], ones_row_f[0:1, :],
                                                    w1row[0:1, 512 * n:512 * (n + 1)],
                                                    start=True, stop=True))],
                          reads=[w1rowB, B_("orf")], writes=[PSB[4 + n]])
                b.do("v", lambda h, n=n: h.tensor_copy(W1B[:, 512 * n:512 * (n + 1)],
                                                       P1[:, 512 * n:512 * (n + 1)]),
                     reads=[PSB[4 + n]], writes=[W1BB])

            wload(wbuf, wbufB, lambda k: wo_a[128 * k:128 * (k + 1), :])
            for tt in range(NTT):
                xb = xb2[tt % 2]; xbB = xb2B[tt % 2]
                b.dma_group("s", [(xb[:, k, :], ctxA[k, :, TTW * tt:TTW * (tt + 1)])
                                  for k in range(NC)], writes=[xbB])
                for m in range(NC):
                    bank = m % 4
                    fns = [(lambda h, k=k, m=m, bank=bank, xb=xb:
                            h.matmul(PSAP[bank], wbuf[:, k, 128 * m:128 * (m + 1)], xb[:, k, :],
                                     start=(k == 0), stop=(k == NC - 1))) for k in range(NC)]
                    b.mmgroup(fns, reads=[wbufB, xbB], writes=[PSB[bank]])
                    eng = "v" if m % 2 == 0 else "s"
                    if eng == "v":
                        b.do("v", lambda h, m=m, bank=bank: h.tensor_copy(EB[:, m, :], PSAP[bank]),
                             reads=[PSB[bank]], writes=[EBB])
                    else:
                        b.do("s", lambda h, m=m, bank=bank: h.activation(EB[:, m, :], PSAP[bank], AF.Copy),
                             reads=[PSB[bank]], writes=[EBB])
                for tb in range(4):
                    t0 = TTW * tt + 128 * tb
                    fns = [(lambda h, c=c, tb=tb: h.matmul(P1[:, 128 * c:128 * (c + 1)],
                                                           EB[:, c, 128 * tb:128 * (tb + 1)], identf,
                                                           start=True, stop=True)) for c in range(NC)]
                    b.mmgroup(fns, reads=[EBB, B_("identf")], writes=PSB[4:8])
                    b.do("v", lambda h: h.tensor_copy(aot, P1[:, :]), reads=PSB[4:8], writes=[aotB])
                    xt = xtok[0]; xtB = xtokB[0]
                    b.dma("y", xt, x_in[t0:t0 + 128, :], writes=[xtB])
                    b.do("v", lambda h: h.reduce_sum(st2[:, 0:1], aot, axis=AX.X),
                         reads=[aotB], writes=[st2B])
                    b.do("s", lambda h: h.activation(f32st[0], aot[:, 0:TTW], AF.Square),
                         reads=[aotB], writes=[f32stB[0]])
                    b.do("s", lambda h: h.activation(f32st[1], aot[:, TTW:2 * TTW], AF.Square),
                         reads=[aotB], writes=[f32stB[1]])
                    b.do("v", lambda h: h.reduce_sum(st2[:, 1:2], f32st[0], axis=AX.X),
                         reads=[f32stB[0]], writes=[st2B])
                    b.do("v", lambda h: h.reduce_sum(st2[:, 2:3], f32st[1], axis=AX.X),
                         reads=[f32stB[1]], writes=[st2B])
                    b.do("s", lambda h: h.activation(f32st[0], aot[:, 2 * TTW:3 * TTW], AF.Square),
                         reads=[aotB], writes=[f32stB[0]])
                    b.do("s", lambda h: h.activation(f32st[1], aot[:, 3 * TTW:4 * TTW], AF.Square),
                         reads=[aotB], writes=[f32stB[1]])
                    b.do("v", lambda h: h.reduce_sum(st2[:, 3:4], f32st[0], axis=AX.X),
                         reads=[f32stB[0]], writes=[st2B])
                    b.do("v", lambda h: h.reduce_sum(st2[:, 4:5], f32st[1], axis=AX.X),
                         reads=[f32stB[1]], writes=[st2B])
                    b.do("v", lambda h: h.tensor_add(st2[:, 1:2], st2[:, 1:2], st2[:, 2:3]),
                         reads=[st2B], writes=[st2B])
                    b.do("v", lambda h: h.tensor_add(st2[:, 3:4], st2[:, 3:4], st2[:, 4:5]),
                         reads=[st2B], writes=[st2B])
                    b.do("v", lambda h: h.tensor_add(st2[:, 1:2], st2[:, 1:2], st2[:, 3:4]),
                         reads=[st2B], writes=[st2B])
                    _ln_apply(st2, st2B, aot, aotB, xt, xtB, t0,
                              W1B, W1BB, hbf, hbfB, hst, hstB)
            b.barrier()

        def _ln_apply(st2, st2B, aot, aotB, xt, xtB, t0,
                      W1B=None, W1BB=None, hbf=None, hbfB=None, hst=None, hstB=None):
            b.do("v", lambda h: h.tensor_scalar(st2[:, 5:6], st2[:, 0:1], 1.0 / D, None, ALU.mult),
                 reads=[st2B], writes=[st2B])
            b.do("v", lambda h: h.tensor_mul(st2[:, 6:7], st2[:, 5:6], st2[:, 5:6]),
                 reads=[st2B], writes=[st2B])
            b.do("v", lambda h: h.scalar_tensor_tensor(st2[:, 7:8], st2[:, 1:2], 1.0 / D, st2[:, 6:7],
                                                       ALU.mult, ALU.subtract),
                 reads=[st2B], writes=[st2B])
            b.do("s", lambda h: h.activation(st2[:, 8:9], st2[:, 7:8], AF.Sqrt, bias=cbias[:, 0:1]),
                 reads=[st2B, B_("cbias")], writes=[st2B])
            b.do("v", lambda h: h.reciprocal(st2[:, 9:10], st2[:, 8:9]), reads=[st2B], writes=[st2B])
            b.do("v", lambda h: h.tensor_scalar(aot, aot, st2[:, 5:6], st2[:, 9:10],
                                                ALU.subtract, ALU.mult),
                 reads=[aotB, st2B], writes=[aotB])
            b.do("v", lambda h: h.tensor_add(xt, xt, aot), reads=[xtB, aotB], writes=[xtB])
            b.dma("y", x1_tok[t0:t0 + 128, :], xt, reads=[xtB])
            b.do("v", lambda h: h.reduce_sum(st2[:, 0:1], xt, axis=AX.X), reads=[xtB], writes=[st2B])
            b.do("s", lambda h: h.activation(aot, xt, AF.Square), reads=[xtB], writes=[aotB])
            b.do("v", lambda h: h.reduce_sum(st2[:, 1:2], aot, axis=AX.X), reads=[aotB], writes=[st2B])
            b.do("v", lambda h: h.tensor_scalar(st2[:, 5:6], st2[:, 0:1], 1.0 / D, None, ALU.mult),
                 reads=[st2B], writes=[st2B])
            b.do("v", lambda h: h.tensor_mul(st2[:, 6:7], st2[:, 5:6], st2[:, 5:6]),
                 reads=[st2B], writes=[st2B])
            b.do("v", lambda h: h.scalar_tensor_tensor(st2[:, 7:8], st2[:, 1:2], 1.0 / D, st2[:, 6:7],
                                                       ALU.mult, ALU.subtract),
                 reads=[st2B], writes=[st2B])
            b.do("s", lambda h: h.activation(st2[:, 8:9], st2[:, 7:8], AF.Sqrt, bias=cbias[:, 0:1]),
                 reads=[st2B, B_("cbias")], writes=[st2B])
            b.do("v", lambda h: h.reciprocal(st2[:, 9:10], st2[:, 8:9]), reads=[st2B], writes=[st2B])
            b.do("v", lambda h: h.tensor_scalar(xt, xt, st2[:, 5:6], st2[:, 9:10],
                                                ALU.subtract, ALU.mult),
                 reads=[xtB, st2B], writes=[xtB])
            fns = [(lambda h, c=c: h.matmul(P1[:, 128 * c:128 * (c + 1)],
                                            xt[:, 128 * c:128 * (c + 1)], identf,
                                            start=True, stop=True)) for c in range(NC)]
            b.mmgroup(fns, reads=[xtB, B_("identf")], writes=PSB[4:8])
            b.do("v", lambda h: h.tensor_copy(aot, P1[:, :]), reads=PSB[4:8], writes=[aotB])
            b.dma_group("y", [(f_inT[c, :, t0:t0 + 128], aot[:, 128 * c:128 * (c + 1)])
                              for c in range(NC)], reads=[aotB])
            b.do("s", lambda h: h.activation(aot, xt, AF.Square), reads=[xtB], writes=[aotB])
            b.do("v", lambda h: h.reduce_sum(st2[:, 10:11], aot, axis=AX.X),
                 reads=[aotB], writes=[st2B])
            b.do("v", lambda h: h.tensor_scalar(st2[:, 11:12], st2[:, 10:11], 1.0 / D, None, ALU.mult),
                 reads=[st2B], writes=[st2B])
            b.do("s", lambda h: h.activation(st2[:, 12:13], st2[:, 11:12], AF.Sqrt, bias=cbias[:, 1:2]),
                 reads=[st2B, B_("cbias")], writes=[st2B])
            b.do("v", lambda h: h.reciprocal(st2[:, 13:14], st2[:, 12:13]), reads=[st2B], writes=[st2B])
            b.do("v", lambda h: h.tensor_scalar(aot, xt, st2[:, 13:14], None, ALU.mult),
                 reads=[xtB, st2B], writes=[aotB])
            b.do("v", lambda h: h.tensor_tensor(hbf, aot, W1B, ALU.mult),
                 reads=[aotB, W1BB], writes=[hbfB])
            fns = [(lambda h, c=c: h.matmul(P1[:, 128 * c:128 * (c + 1)],
                                            hbf[:, 128 * c:128 * (c + 1)], ident,
                                            start=True, stop=True)) for c in range(NC)]
            b.mmgroup(fns, reads=[hbfB, B_("ident")], writes=PSB[4:8])
            b.do("s", lambda h: h.activation(hst, P1[:, :], AF.Copy), reads=PSB[4:8], writes=[hstB])
            b.dma_group("y", [(hT[c, :, t0:t0 + 128], hst[:, 128 * c:128 * (c + 1)])
                              for c in range(NC)], reads=[hstB])

        def phase_p6():
            ABF.reset()
            xb2 = [ABF.take([128, NC, TTW]) for _ in range(2)]
            xb2B = [B_("p6x%d" % i) for i in range(2)]
            h2b2 = [ABF.take([128, TTW]) for _ in range(2)]
            h2b2B = [B_("p6h%d" % i) for i in range(2)]
            sqb = ABF.take([128, TTW]); sqbB = B_("p6sq")
            ABF2.reset()
            wbuf = ABF2.take([128, NC, D]); wbufB = B_("p6w")
            AF32.reset()
            X1SB = AF32.take([128, NC, TTW])
            X1SBB = [B_("p6x1_%d" % i) for i in range(NC)]
            hf32 = AF32.take([128, TTW]); hf32B = B_("p6hf")
            rstdb = AF32.take([128, TTW]); rstdbB = B_("p6rstd")
            ftile = [AF32.take([128, TTW]) for _ in range(2)]
            ftileB = [B_("p6f%d" % i) for i in range(2)]
            AF32X.reset()
            strow = AF32X.take([1, TTW]); strowB = B_("p6st")
            rsm = AF32X.take([128, 40]); rsmB = B_("p6rsm")
            combst = AF32X.take([8, 128]); combstB = B_("p6comb")
            egrow = AF32X.take([1, TTW]); egrowB = B_("p6eg")

            wload(wbuf, wbufB, lambda k: wo_f[128 * k:128 * (k + 1), :])
            for tt in range(NTT):
                xb = xb2[tt % 2]; xbB = xb2B[tt % 2]
                b.dma_group("s", [(xb[:, k, :], ctxF[k, :, TTW * tt:TTW * (tt + 1)])
                                  for k in range(NC)], writes=[xbB])
                for m in range(NC):
                    bank = m % 4
                    ft = ftile[m % 2]; ftB = ftileB[m % 2]
                    b.dma("y", ft, f_inT[m, :, TTW * tt:TTW * (tt + 1)], writes=[ftB])
                    fns = [(lambda h, k=k, m=m, bank=bank, xb=xb:
                            h.matmul(PSAP[bank], wbuf[:, k, 128 * m:128 * (m + 1)], xb[:, k, :],
                                     start=(k == 0), stop=(k == NC - 1))) for k in range(NC)]
                    b.mmgroup(fns, reads=[wbufB, xbB], writes=[PSB[bank]])
                    b.do("v", lambda h, m=m, bank=bank, ft=ft:
                         h.tensor_add(X1SB[:, m, :], PSAP[bank], ft),
                         reads=[PSB[bank], ftB], writes=[X1SBB[m]])
                    b.dma("y", x1fT[m, :, TTW * tt:TTW * (tt + 1)], X1SB[:, m, :], reads=[X1SBB[m]])
                    b.do("s", lambda h, m=m: h.activation(sqb, X1SB[:, m, :], AF.Square),
                         reads=[X1SBB[m]], writes=[sqbB])
                    grp = b.mmgroup([(lambda h, m=m: h.matmul(P1[0:1, 0:TTW], onesb[:, 0:1], sqb,
                                                               start=(m == 0), stop=(m == NC - 1)))],
                                    reads=[sqbB, B_("onesb")],
                                    war=([PSB[4]] if m == 0 else ()),
                                    writes=([PSB[4]] if m == NC - 1 else ()))
                b.do("v", lambda h: h.tensor_copy(strow, P1[0:1, 0:TTW]), reads=[PSB[4]], writes=[strowB])
                b.do("v", lambda h: h.tensor_scalar(strow, strow, 1.0 / D, None, ALU.mult),
                     reads=[strowB], writes=[strowB])
                b.do("s", lambda h: h.activation(strow, strow, AF.Sqrt, bias=cbias[0:1, 1:2]),
                     reads=[strowB, B_("cbias")], writes=[strowB])
                b.do("v", lambda h: h.reciprocal(strow, strow), reads=[strowB], writes=[strowB])
                b.mmgroup([lambda h: h.matmul(P1[:, 1536:1536 + TTW], ones_row_f[0:1, :], strow,
                                              start=True, stop=True)],
                          reads=[strowB, B_("orf")], writes=[PSB[7]])
                b.do("v", lambda h: h.tensor_copy(rstdb, P1[:, 1536:1536 + TTW]),
                     reads=[PSB[7]], writes=[rstdbB])
                for m in range(NC):
                    hb = h2b2[m % 2]; hbB = h2b2B[m % 2]
                    b.do("v", lambda h, m=m: h.scalar_tensor_tensor(
                            hf32, X1SB[:, m, :], lnw2[:, m:m + 1], rstdb, ALU.mult, ALU.mult),
                         reads=[X1SBB[m], rstdbB, B_("lnw2")], writes=[hf32B])
                    b.do("s", lambda h, hb=hb: h.activation(hb, hf32, AF.Copy),
                         reads=[hf32B], writes=[hbB])
                    b.dma("y", h2T[m, :, TTW * tt:TTW * (tt + 1)], hb, reads=[hbB])
                    fns = [(lambda h, tb=tb, m=m: h.matmul(
                            PSAP[tb][:, 0:8],
                            hf32[:, 128 * tb:128 * (tb + 1)], wrt[:, m, :],
                            start=(m == 0), stop=(m == NC - 1))) for tb in range(4)]
                    b.mmgroup(fns, reads=[hf32B, B_("wrt")],
                              war=(PSB[0:4] if m == 0 else ()),
                              writes=(PSB[0:4] if m == NC - 1 else ()))
                    b.mmgroup([(lambda h, m=m, hb=hb: h.matmul(P1[0:1, 1024:1024 + TTW],
                                                               weg[:, m:m + 1], hb,
                                                               start=(m == 0), stop=(m == NC - 1)))],
                              reads=[hbB, B_("weg")],
                              war=([PSB[6]] if m == 0 else ()),
                              writes=([PSB[6]] if m == NC - 1 else ()))
                for tb in range(4):
                    b.do("v", lambda h, tb=tb: h.tensor_copy(rsm[:, 0:8], PSAP[tb][:, 0:8]),
                         reads=[PSB[tb]], writes=[rsmB])
                    b.do("v", lambda h: h.reduce_max(rsm[:, 8:9], rsm[:, 0:8], axis=AX.X, negate=True),
                         reads=[rsmB], writes=[rsmB])
                    b.do("s", lambda h: h.activation(rsm[:, 0:8], rsm[:, 0:8], AF.Exp,
                                                     bias=rsm[:, 8:9], accum_out=rsm[:, 9:10]),
                         reads=[rsmB], writes=[rsmB])
                    b.do("v", lambda h: h.reciprocal(rsm[:, 10:11], rsm[:, 9:10]),
                         reads=[rsmB], writes=[rsmB])
                    b.do("v", lambda h: h.tensor_scalar(rsm[:, 0:8], rsm[:, 0:8], rsm[:, 10:11],
                                                        None, ALU.mult),
                         reads=[rsmB], writes=[rsmB])
                    b.do("v", lambda h: h.reduce_max(rsm[:, 11:12], rsm[:, 0:8], axis=AX.X),
                         reads=[rsmB], writes=[rsmB])
                    b.do("v", lambda h: h.tensor_scalar(rsm[:, 16:24], rsm[:, 0:8], rsm[:, 11:12],
                                                        None, ALU.is_ge),
                         reads=[rsmB], writes=[rsmB])
                    b.do("v", lambda h: h.scalar_tensor_tensor(rsm[:, 16:24], rsm[:, 16:24], NEG_BIG,
                                                               rsm[:, 0:8], ALU.mult, ALU.add),
                         reads=[rsmB], writes=[rsmB])
                    b.do("v", lambda h: h.reduce_max(rsm[:, 12:13], rsm[:, 16:24], axis=AX.X),
                         reads=[rsmB], writes=[rsmB])
                    b.do("v", lambda h: h.tensor_scalar(rsm[:, 16:24], rsm[:, 0:8], rsm[:, 12:13],
                                                        None, ALU.is_ge),
                         reads=[rsmB], writes=[rsmB])
                    b.do("v", lambda h: h.tensor_mul(rsm[:, 24:32], rsm[:, 0:8], rsm[:, 16:24]),
                         reads=[rsmB], writes=[rsmB])
                    b.mmgroup([(lambda h: h.matmul(P1[0:8, 512:640], rsm[:, 24:32], identf,
                                                   start=True, stop=True))],
                              reads=[rsmB, B_("identf")], writes=[PSB[5]])
                    b.do("v", lambda h: h.tensor_copy(combst, P1[0:8, 512:640]),
                         reads=[PSB[5]], writes=[combstB])
                    t0 = TTW * tt + 128 * tb
                    b.dma("y", combT[:, t0:t0 + 128], combst, reads=[combstB])
                b.do("s", lambda h: h.activation(egrow, P1[0:1, 1024:1024 + TTW], AF.Sigmoid),
                     reads=[PSB[6]], writes=[egrowB])
                b.dma("y", sig_eg_d[0:1, TTW * tt:TTW * (tt + 1)], egrow, reads=[egrowB])
            b.barrier()

        def phase_p7a():
            MW = 256
            nmt = T // MW
            ABF.reset()
            wu = ABF.take([128, NC, MOE_I]); wuB = B_("p7wu")
            xb2 = [ABF.take([128, NC, MW]) for _ in range(2)]
            xb2B = [B_("p7x%d" % i) for i in range(2)]
            ABF2.reset()
            wg = ABF2.take([128, NC, MOE_I]); wgB = B_("p7wg")
            wcst = [ABF2.take([128, D]) for _ in range(2)]
            wcstB = [B_("p7wc%d" % i) for i in range(2)]
            midb2 = [ABF2.take([128, MW]) for _ in range(2)]
            midb2B = [B_("p7m%d" % i) for i in range(2)]
            AF32.reset()
            sg = AF32.take([128, MW]); sgB = B_("p7sg")
            combb = AF32.take([128, MW]); combbB = B_("p7cb")
            crow = AF32.take([1, T]); crowB = B_("p7crow")

            def precast(lo, hi):
                for kidx in range(lo, min(hi, NE * NMC + NSC)):
                    wc = wcst[kidx % 2]; wcB = wcstB[kidx % 2]
                    if kidx < NE * NMC:
                        ee, mc = kidx // NMC, kidx % NMC
                        csrc = w_down_e[ee, 128 * mc:128 * (mc + 1), :]
                    else:
                        sc = kidx - NE * NMC
                        csrc = w_sh_down[128 * sc:128 * (sc + 1), :]
                    b.dma("g", wc, csrc, writes=[wcB])
                    b.dma("y", wdn_bf[kidx, :, :], wc, reads=[wcB])

            for e in range(NE):
                precast(17 * e, 17 * (e + 1) if e < NE - 1 else NE * NMC + NSC)
                b.dma_group("g", [(wg[:, k, :], w_gate_e[e, 128 * k:128 * (k + 1), :])
                                  for k in range(NC)], writes=[wgB])
                b.dma_group("g", [(wu[:, k, :], w_up_e[e, 128 * k:128 * (k + 1), :])
                                  for k in range(NC)], writes=[wuB])
                b.dma("s", crow, combT[e:e + 1, :], writes=[crowB])
                for ttt in range(nmt):
                    xb = xb2[ttt % 2]; xbB = xb2B[ttt % 2]
                    b.dma_group("s", [(xb[:, k, :], h2T[k, :, MW * ttt:MW * (ttt + 1)])
                                      for k in range(NC)], writes=[xbB])
                    b.mmgroup([(lambda h, ttt=ttt: h.matmul(P1[:, 1536:1536 + MW], ones_row_f[0:1, :],
                                                            crow[0:1, MW * ttt:MW * (ttt + 1)],
                                                            start=True, stop=True))],
                              reads=[crowB, B_("orf")], writes=[PSB[7]])
                    b.do("v", lambda h: h.tensor_copy(combb, P1[:, 1536:1536 + MW]),
                         reads=[PSB[7]], writes=[combbB])
                    for mc in range(NMC):
                        bk_g = mc % 2
                        bk_u = 2 + mc % 2
                        fns = [(lambda h, k=k, mc=mc, bk_g=bk_g, xb=xb:
                                h.matmul(PSAP[bk_g][:, 0:MW], wg[:, k, 128 * mc:128 * (mc + 1)],
                                         xb[:, k, :], start=(k == 0), stop=(k == NC - 1)))
                               for k in range(NC)]
                        b.mmgroup(fns, reads=[wgB, xbB], writes=[PSB[bk_g]])
                        fns = [(lambda h, k=k, mc=mc, bk_u=bk_u, xb=xb:
                                h.matmul(PSAP[bk_u][:, 0:MW], wu[:, k, 128 * mc:128 * (mc + 1)],
                                         xb[:, k, :], start=(k == 0), stop=(k == NC - 1)))
                               for k in range(NC)]
                        b.mmgroup(fns, reads=[wuB, xbB], writes=[PSB[bk_u]])
                        mb = midb2[mc % 2]; mbB = midb2B[mc % 2]
                        b.do("s", lambda h, bk_g=bk_g: h.activation(sg, PSAP[bk_g][:, 0:MW], AF.Silu),
                             reads=[PSB[bk_g]], writes=[sgB])
                        b.do("v", lambda h: h.tensor_mul(sg, sg, combb),
                             reads=[sgB, combbB], writes=[sgB])
                        b.do("v", lambda h, bk_u=bk_u, mb=mb: h.tensor_tensor(
                                mb, sg, PSAP[bk_u][:, 0:MW], ALU.mult),
                             reads=[sgB, PSB[bk_u]], writes=[mbB])
                        b.dma("y", midT[e, mc, :, MW * ttt:MW * (ttt + 1)], mb, reads=[mbB])
            b.barrier()

        def phase_p8a():
            NSTR = SH_I // TTW
            ABF.reset()
            xb2 = [ABF.take([128, NC, TTW]) for _ in range(2)]
            xb2B = [B_("p8x%d" % i) for i in range(2)]
            smb2 = [ABF.take([128, TTW]) for _ in range(2)]
            smb2B = [B_("p8m%d" % i) for i in range(2)]
            ABF2.reset()
            wg2 = [ABF2.take([128, NC, TTW]) for _ in range(2)]
            wg2B = [B_("p8wg%d" % i) for i in range(2)]
            wu2 = [ABF2.take([128, NC, TTW]) for _ in range(2)]
            wu2B = [B_("p8wu%d" % i) for i in range(2)]
            AF32.reset()
            sg = AF32.take([128, TTW]); sgB = B_("p8sg")
            srow = AF32.take([1, T]); srowB = B_("p8srow")
            AF32X.reset()
            sigb = AF32X.take([128, NTT, TTW]); sigbB = B_("p8sigb")

            b.dma("s", srow, sig_eg_d[0:1, :], writes=[srowB])
            for tt in range(NTT):
                b.mmgroup([(lambda h, tt=tt: h.matmul(P1[:, 1536:1536 + TTW], ones_row_f[0:1, :],
                                                      srow[0:1, TTW * tt:TTW * (tt + 1)],
                                                      start=True, stop=True))],
                          reads=[srowB, B_("orf")], writes=[PSB[7]])
                b.do("v", lambda h, tt=tt: h.tensor_copy(sigb[:, tt, :], P1[:, 1536:1536 + TTW]),
                     reads=[PSB[7]], writes=[sigbB])

            for si in range(NSTR):
                par = si % 2
                wgs = wg2[par]; wgsB = wg2B[par]
                wus = wu2[par]; wusB = wu2B[par]
                b.dma_group("g", [(wgs[:, k, :], w_sh_gate[128 * k:128 * (k + 1),
                                                           TTW * si:TTW * (si + 1)])
                                  for k in range(NC)], writes=[wgsB])
                b.dma_group("g", [(wus[:, k, :], w_sh_up[128 * k:128 * (k + 1),
                                                         TTW * si:TTW * (si + 1)])
                                  for k in range(NC)], writes=[wusB])
                for tt in range(NTT):
                    xb = xb2[tt % 2]; xbB = xb2B[tt % 2]
                    b.dma_group("s", [(xb[:, k, :], h2T[k, :, TTW * tt:TTW * (tt + 1)])
                                      for k in range(NC)], writes=[xbB])
                    for ms in range(4):
                        bk_g = ms % 2
                        bk_u = 2 + ms % 2
                        fns = [(lambda h, k=k, ms=ms, bk_g=bk_g, xb=xb, wgs=wgs:
                                h.matmul(PSAP[bk_g], wgs[:, k, 128 * ms:128 * (ms + 1)],
                                         xb[:, k, :], start=(k == 0), stop=(k == NC - 1)))
                               for k in range(NC)]
                        b.mmgroup(fns, reads=[wgsB, xbB], writes=[PSB[bk_g]])
                        fns = [(lambda h, k=k, ms=ms, bk_u=bk_u, xb=xb, wus=wus:
                                h.matmul(PSAP[bk_u], wus[:, k, 128 * ms:128 * (ms + 1)],
                                         xb[:, k, :], start=(k == 0), stop=(k == NC - 1)))
                               for k in range(NC)]
                        b.mmgroup(fns, reads=[wusB, xbB], writes=[PSB[bk_u]])
                        mb = smb2[ms % 2]; mbB = smb2B[ms % 2]
                        b.do("s", lambda h, bk_g=bk_g: h.activation(sg, PSAP[bk_g], AF.Silu),
                             reads=[PSB[bk_g]], writes=[sgB])
                        b.do("v", lambda h, tt=tt: h.tensor_mul(sg, sg, sigb[:, tt, :]),
                             reads=[sgB, sigbB], writes=[sgB])
                        b.do("v", lambda h, bk_u=bk_u, mb=mb: h.tensor_tensor(
                                mb, sg, PSAP[bk_u], ALU.mult),
                             reads=[sgB, PSB[bk_u]], writes=[mbB])
                        b.dma("y", smidT[4 * si + ms, :, TTW * tt:TTW * (tt + 1)], mb, reads=[mbB])
            b.barrier()

        def phase_p8b():
            NK = NE * NMC + NSC
            ABF.reset()
            rh4 = [ABF.take([128, TTW]) for _ in range(4)]
            rh4B = [B_("pbr%d" % i) for i in range(4)]
            ws4 = [ABF.take([128, 1024]) for _ in range(4)]
            ws4B = [B_("pbw%d" % i) for i in range(4)]
            AF32.reset()
            xf2 = [AF32.take([128, TTW]) for _ in range(2)]
            xf2B = [B_("pbx%d" % i) for i in range(2)]
            ys2 = [AF32.take([128, TTW]) for _ in range(2)]
            ys2B = [B_("pby%d" % i) for i in range(2)]

            def ksrc(kidx, tt):
                if kidx < NE * NMC:
                    e, mc = kidx // NMC, kidx % NMC
                    return midT[e, mc, :, TTW * tt:TTW * (tt + 1)]
                return smidT[kidx - NE * NMC, :, TTW * tt:TTW * (tt + 1)]

            for mh in range(2):
                for tt in range(NTT):
                    for kidx in range(NK):
                        rh = rh4[kidx % 4]; rhB = rh4B[kidx % 4]
                        ws = ws4[kidx % 4]; wsB = ws4B[kidx % 4]
                        b.dma("s", rh, ksrc(kidx, tt), writes=[rhB])
                        b.dma("g", ws, wdn_bf[kidx, :, 1024 * mh:1024 * (mh + 1)], writes=[wsB])
                        fns = [(lambda h, m=m, rh=rh, ws=ws, kidx=kidx:
                                h.matmul(PSAP[m], ws[:, 128 * m:128 * (m + 1)], rh,
                                         start=(kidx == 0), stop=(kidx == NK - 1)))
                               for m in range(8)]
                        b.mmgroup(fns, reads=[rhB, wsB],
                                  war=(PSB if kidx == 0 else ()),
                                  writes=(PSB if kidx == NK - 1 else ()))
                    for m in range(8):
                        xf = xf2[m % 2]; xfB = xf2B[m % 2]
                        ys = ys2[m % 2]; ysB = ys2B[m % 2]
                        b.dma("y", xf, x1fT[8 * mh + m, :, TTW * tt:TTW * (tt + 1)], writes=[xfB])
                        b.do("v", lambda h, m=m, xf=xf, ys=ys: h.tensor_add(ys, PSAP[m], xf),
                             reads=[PSB[m], xfB], writes=[ysB])
                        b.dma("y", y2T[8 * mh + m, :, TTW * tt:TTW * (tt + 1)], ys, reads=[ysB])
            b.barrier()

        def phase_final():
            AF32.reset()
            y2c = AF32.take([128, NC, 128]); y2cB = B_("p9c")
            y2tok = AF32.take([128, D]); y2tokB = B_("p9t")
            x1t = AF32.take([128, D]); x1tB = B_("p9x")
            sq2 = AF32.take([128, D]); sq2B = B_("p9sq")
            outsb = AF32.take([128, D]); outsbB = B_("p9o")
            AF32X.reset()
            st2 = AF32X.take([128, 16]); st2B = B_("p9st")
            for tb in range(T // 128):
                b.dma_group("y", [(y2c[:, c, :], y2T[c, :, 128 * tb:128 * (tb + 1)])
                                  for c in range(NC)], writes=[y2cB])
                fns = [(lambda h, c=c: h.matmul(P1[:, 128 * c:128 * (c + 1)],
                                                y2c[:, c, :], identf, start=True, stop=True))
                       for c in range(NC)]
                b.mmgroup(fns, reads=[y2cB, B_("identf")], writes=PSB[4:8])
                b.do("v", lambda h: h.tensor_copy(y2tok, P1[:, :]), reads=PSB[4:8], writes=[y2tokB])
                b.dma("s", x1t, x1_tok[128 * tb:128 * (tb + 1), :], writes=[x1tB])
                b.do("v", lambda h: h.reduce_sum(st2[:, 0:1], y2tok, axis=AX.X),
                     reads=[y2tokB], writes=[st2B])
                b.do("s", lambda h: h.activation(sq2, y2tok, AF.Square), reads=[y2tokB], writes=[sq2B])
                b.do("v", lambda h: h.reduce_sum(st2[:, 1:2], sq2, axis=AX.X),
                     reads=[sq2B], writes=[st2B])
                b.do("v", lambda h: h.tensor_scalar(st2[:, 2:3], st2[:, 0:1], 1.0 / D, None, ALU.mult),
                     reads=[st2B], writes=[st2B])
                b.do("v", lambda h: h.tensor_mul(st2[:, 3:4], st2[:, 2:3], st2[:, 2:3]),
                     reads=[st2B], writes=[st2B])
                b.do("v", lambda h: h.scalar_tensor_tensor(st2[:, 4:5], st2[:, 1:2], 1.0 / D,
                                                           st2[:, 3:4], ALU.mult, ALU.subtract),
                     reads=[st2B], writes=[st2B])
                b.do("s", lambda h: h.activation(st2[:, 5:6], st2[:, 4:5], AF.Sqrt, bias=cbias[:, 0:1]),
                     reads=[st2B, B_("cbias")], writes=[st2B])
                b.do("v", lambda h: h.reciprocal(st2[:, 6:7], st2[:, 5:6]), reads=[st2B], writes=[st2B])
                b.do("v", lambda h: h.tensor_scalar(y2tok, y2tok, st2[:, 2:3], st2[:, 6:7],
                                                    ALU.subtract, ALU.mult),
                     reads=[y2tokB, st2B], writes=[y2tokB])
                b.do("v", lambda h: h.tensor_add(outsb, y2tok, x1t),
                     reads=[y2tokB, x1tB], writes=[outsbB])
                b.dma("y", out[128 * tb:128 * (tb + 1), :], outsb, reads=[outsbB])
            b.barrier()

        setup_phase()
        b.barrier()
        order = [("ln0", phase_ln0),
                 ("qkvA", lambda: phase_qkv("A")),
                 ("attnA", lambda: phase_attn("A")),
                 ("p3", phase_p3),
                 ("qkvF", lambda: phase_qkv("F")),
                 ("attnF", lambda: phase_attn("F")),
                 ("p6", phase_p6),
                 ("p7a", phase_p7a),
                 ("p8a", phase_p8a),
                 ("p8b", phase_p8b),
                 ("final", phase_final)]
        for name, fn in order:
            fn()
            if stop_after == name:
                break

        b.barrier()
        _play(nc, b)
    return nc


def _play(nc, b):
    with nc.Block() as block:
        @block.tensor
        def _(h):
            for f in b.ops["t"]:
                f(h)

        @block.vector
        def _(h):
            for f in b.ops["v"]:
                f(h)

        @block.scalar
        def _(h):
            for f in b.ops["s"]:
                f(h)

        @block.gpsimd
        def _(h):
            for f in b.ops["g"]:
                f(h)

        @block.sync
        def _(h):
            for f in b.ops["y"]:
                f(h)



def _prep_inputs(inp):
    f32 = lambda a: np.ascontiguousarray(np.asarray(a, dtype=np.float32))
    g = {}
    g["attn_Wq"] = f32(inp["L0_attn_Wq"]); g["attn_Wk"] = f32(inp["L0_attn_Wk"])
    g["attn_Wv"] = f32(inp["L0_attn_Wv"]); g["attn_Wo"] = f32(inp["L0_attn_Wo"])
    g["splat_pos"] = f32(inp["L0_attn_splat_positions"]).reshape(128, 128)
    g["splat_dir"] = f32(inp["L0_attn_splat_directions"]).reshape(128, 128)
    g["splat_lsc"] = f32(inp["L0_attn_splat_log_scales"]).reshape(128, 1)
    g["splat_lam"] = f32(inp["L0_attn_splat_log_amplitudes"]).reshape(16, 8)
    g["splat_ds"] = f32(inp["L0_attn_directional_strength"]).reshape(1, 1)
    g["splat_gsa"] = f32(inp["L0_attn_gsa_strength"]).reshape(1, 1)
    g["w_ln1"] = _fm_vec(inp["L0_ffn_w_ln1"]); g["w_ln2"] = _fm_vec(inp["L0_ffn_w_ln2"])
    g["w_ln1_row"] = f32(inp["L0_ffn_w_ln1"]).reshape(1, D)
    g["ffn_bq"] = _fm_vec(inp["L0_ffn_bq"]); g["ffn_bk"] = _fm_vec(inp["L0_ffn_bk"])
    g["ffn_bv"] = f32(inp["L0_ffn_bv"]).reshape(1, D)
    g["ffn_wq"] = f32(inp["L0_ffn_wq"]); g["ffn_wk"] = f32(inp["L0_ffn_wk"])
    g["ffn_wv"] = f32(inp["L0_ffn_wv"]); g["ffn_wo"] = f32(inp["L0_ffn_wo"])
    g["w_router"] = f32(inp["L0_ffn_w_router"])
    g["w_gate_e"] = f32(inp["L0_ffn_w_gate_e"])
    g["w_up_e"] = f32(inp["L0_ffn_w_up_e"])
    g["w_down_e"] = f32(inp["L0_ffn_w_down_e"])
    g["w_sh_gate"] = f32(inp["L0_ffn_w_sh_gate"])
    g["w_sh_up"] = f32(inp["L0_ffn_w_sh_up"])
    g["w_sh_down"] = f32(inp["L0_ffn_w_sh_down"])
    g["w_sh_eg"] = _fm_vec(inp["L0_ffn_w_sh_eg"])
    g.update(_host_constants())
    return g


def kernel(**inp):
    nc = build_program()
    shared = _prep_inputs(inp)
    x = np.ascontiguousarray(np.asarray(inp["L0_attn_hidden_states"], dtype=np.float32))
    in_maps = []
    for c in range(NCORES):
        m = dict(shared)
        m["x"] = np.ascontiguousarray(x[SEQ_PC * c:SEQ_PC * (c + 1)].reshape(T, D))
        in_maps.append(m)
    res = run_bass_kernel_spmd(nc, in_maps, core_ids=list(range(NCORES)))
    parts = [res.results[c]["out"].reshape(SEQ_PC, S, D) for c in range(NCORES)]
    return np.ascontiguousarray(np.concatenate(parts, axis=0), dtype=np.float32)


# version 02
# speedup vs baseline: 1.2746x; 1.2746x over previous
import numpy as np
import ml_dtypes
from contextlib import ExitStack

import concourse.bass as bass
import concourse.mybir as mybir
from concourse.bass_utils import run_bass_kernel_spmd

F32 = mybir.dt.float32
BF16 = mybir.dt.bfloat16
AF = mybir.ActivationFunctionType
ALU = mybir.AluOpType
AX = mybir.AxisListType

B, S, D = 16, 2048, 2048
NH, HD, NS = 16, 128, 8
NE, MOE_I, SH_I = 8, 1408, 5632
NCORES = 8
SEQ_PC = 2
T = SEQ_PC * S
NC = D // 128
TTW = 512
NTT = T // TTW
NMC = MOE_I // 128
NSC = SH_I // 128
EPS_LN = 1e-5
EPS_RMS = 1e-6
ROPE_THETA = 1e6
INV_SQRT_HD = float(1.0 / np.sqrt(128.0))
NEG_BIG = -1e30

ENGINES = ("t", "v", "s", "g", "y")
ESEM = {"t": "st", "v": "sv", "s": "ss", "g": "sg", "y": "sy"}
DSEM = {"y": "dy", "g": "dg", "s": "ds"}


class Buf:
    __slots__ = ("w", "r")

    def __init__(self):
        self.w = None
        self.r = []


class Bld:
    def __init__(self, nc, stack):
        self.nc = nc
        self.ops = {e: [] for e in ENGINES}
        self.sem = {}
        self.cnt = {}
        self.lastwait = {e: {} for e in ENGINES}
        for name in ("st", "sv", "ss", "sg", "sy", "dy", "dg", "ds"):
            self.sem[name] = stack.enter_context(nc.semaphore(name))
            self.cnt[name] = 0

    def wait(self, e, tok):
        if tok is None:
            return
        sname, val = tok
        lw = self.lastwait[e]
        if lw.get(sname, 0) >= val:
            return
        lw[sname] = val
        sem = self.sem[sname]
        self.ops[e].append(lambda h, sem=sem, val=val: h.wait_ge(sem, val))

    def _deps(self, e, reads, writes):
        for b_ in reads:
            self.wait(e, b_.w)
        for b_ in writes:
            self.wait(e, b_.w)
            for t in b_.r:
                self.wait(e, t)

    def _record(self, tok, reads, writes):
        if tok is not None:
            for b_ in reads:
                b_.r.append(tok)
        for b_ in writes:
            b_.w = tok
            b_.r = []

    def do(self, e, fn, reads=(), writes=(), inc=True):
        self._deps(e, reads, writes)
        sname = ESEM[e]
        if inc:
            self.cnt[sname] += 1
            val = self.cnt[sname]
            sem = self.sem[sname]
            self.ops[e].append(lambda h, fn=fn, sem=sem: fn(h).then_inc(sem, 1))
            tok = (sname, val)
        else:
            self.ops[e].append(lambda h, fn=fn: fn(h))
            tok = None
        self._record(tok, reads, writes)
        return tok

    def mmgroup(self, fns, reads=(), writes=(), war=()):
        self._deps("t", reads, list(writes) + list(war))
        sem = self.sem["st"]
        for fn in fns[:-1]:
            self.ops["t"].append(lambda h, fn=fn: fn(h))
        self.cnt["st"] += 1
        val = self.cnt["st"]
        fn = fns[-1]
        self.ops["t"].append(lambda h, fn=fn, sem=sem: fn(h).then_inc(sem, 1))
        tok = ("st", val)
        self._record(tok, reads, writes)
        return tok

    def dma(self, qe, out, in_, reads=(), writes=(), **kw):
        return self.dma_group(qe, [(out, in_)], reads=reads, writes=writes, **kw)

    def dma_group(self, qe, items, reads=(), writes=(), **kw):
        self._deps(qe, reads, writes)
        sname = DSEM[qe]
        sem = self.sem[sname]
        for out, in_ in items:
            self.cnt[sname] += 16
            self.ops[qe].append(
                lambda h, out=out, in_=in_, sem=sem, kw=kw:
                    h.dma_start(out, in_, **kw).then_inc(sem, 16))
        tok = (sname, self.cnt[sname])
        self._record(tok, reads, writes)
        return tok

    def barrier(self):
        marks = [(sn, self.cnt[sn]) for sn in self.cnt]
        for e in ENGINES:
            for m in marks:
                self.wait(e, m)


class Carver:

    def __init__(self, handle, nelem):
        self.h = handle
        self.n = nelem
        self.off = 0
        self.hwm = 0

    def take(self, shape):
        p = shape[0]
        free = shape[1:]
        cnt = int(np.prod(free))
        assert self.off + cnt <= self.n, f"arena overflow {self.off}+{cnt} > {self.n}"
        ap = self.h[0:p, self.off:self.off + cnt]
        if len(free) > 1:
            names = "abcdef"[:len(free)]
            pat = "p (" + " ".join(names) + ") -> p " + " ".join(names)
            ap = ap.rearrange(pat, **dict(zip(names, free)))
        self.off += cnt
        self.hwm = max(self.hwm, self.off)
        return ap

    def reset(self, off=0):
        self.off = off



def _host_constants():
    bf = ml_dtypes.bfloat16
    c = {}
    c["c_ident_bf"] = np.eye(128, dtype=bf)
    c["c_ident_f32"] = np.eye(128, dtype=np.float32)
    c["c_ones_bf"] = np.ones((128, 16), dtype=bf)
    c["c_ones_f32"] = np.ones((128, 16), dtype=np.float32)
    c["c_ones_row_bf"] = np.ones((1, 128), dtype=bf)
    rep = np.zeros((16, 128), dtype=np.float32)
    for hh in range(16):
        rep[hh, 8 * hh:8 * hh + 8] = 1.0
    c["c_rep"] = rep
    c["c_ones_row_f32"] = np.ones((1, 128), dtype=np.float32)
    m = np.zeros((128, 128), dtype=np.float32)
    m[np.triu_indices(128, 1)] = NEG_BIG
    c["c_mask"] = m
    inv = 1.0 / (ROPE_THETA ** (np.arange(0, HD, 2, dtype=np.float64) / HD))
    freqs = np.arange(S, dtype=np.float64)[:, None] * inv[None, :]
    emb = np.concatenate([freqs, freqs], axis=-1)
    c["c_ropecos"] = np.ascontiguousarray(np.cos(emb).T.astype(bf))
    c["c_ropesin"] = np.ascontiguousarray(np.sin(emb).T.astype(bf))
    return c


def _fm_vec(v):
    return np.ascontiguousarray(np.asarray(v, np.float32).reshape(NC, 128).T)



def build_program(debug_taps=(), stop_after=None):
    taps = set(debug_taps)
    nc = bass.Bass("TRN2", target_bir_lowering=False)

    def dram(name, shape, dtype=F32, kind="ExternalInput"):
        return nc.dram_tensor(name, shape, dtype, kind=kind).ap()

    def scratch(name, shape, dtype):
        kind = "ExternalOutput" if name in taps else "Internal"
        return nc.dram_tensor(name, shape, dtype, kind=kind).ap()

    x_in = dram("x", [T, D])
    wq_a = dram("attn_Wq", [D, D]); wk_a = dram("attn_Wk", [D, D])
    wv_a = dram("attn_Wv", [D, D]); wo_a = dram("attn_Wo", [D, D])
    sp_pos = dram("splat_pos", [128, 128])
    sp_dir = dram("splat_dir", [128, 128])
    sp_lsc = dram("splat_lsc", [128, 1])
    sp_lam = dram("splat_lam", [16, 8])
    sp_ds = dram("splat_ds", [1, 1])
    sp_gsa = dram("splat_gsa", [1, 1])
    w_ln1 = dram("w_ln1", [128, 16]); w_ln2 = dram("w_ln2", [128, 16])
    w_ln1_row = dram("w_ln1_row", [1, D])
    bq_f = dram("ffn_bq", [128, 16]); bk_f = dram("ffn_bk", [128, 16])
    bv_f = dram("ffn_bv", [1, D])
    wq_f = dram("ffn_wq", [D, D]); wk_f = dram("ffn_wk", [D, D])
    wv_f = dram("ffn_wv", [D, D]); wo_f = dram("ffn_wo", [D, D])
    w_router = dram("w_router", [D, NE])
    w_gate_e = dram("w_gate_e", [NE, D, MOE_I])
    w_up_e = dram("w_up_e", [NE, D, MOE_I])
    w_down_e = dram("w_down_e", [NE, MOE_I, D])
    w_sh_gate = dram("w_sh_gate", [D, SH_I])
    w_sh_up = dram("w_sh_up", [D, SH_I])
    w_sh_down = dram("w_sh_down", [SH_I, D])
    w_sh_eg = dram("w_sh_eg", [128, 16])
    id_bf_d = dram("c_ident_bf", [128, 128], BF16)
    id_f32_d = dram("c_ident_f32", [128, 128])
    ones_bf_d = dram("c_ones_bf", [128, 16], BF16)
    ones_f32_d = dram("c_ones_f32", [128, 16])
    ones_row_bf_d = dram("c_ones_row_bf", [1, 128], BF16)
    ones_row_f32_d = dram("c_ones_row_f32", [1, 128])
    rep_d = dram("c_rep", [16, 128])
    mask_d = dram("c_mask", [128, 128])
    ropecos_d = dram("c_ropecos", [128, S], BF16)
    ropesin_d = dram("c_ropesin", [128, S], BF16)

    out = dram("out", [T, D], F32, kind="ExternalOutput")
    ln0T = scratch("ln0T", [NC, 128, T], BF16)
    qA = scratch("qA", [NC, 128, T], BF16)
    kA = scratch("kA", [NC, 128, T], BF16)
    vA = scratch("vA", [SEQ_PC, NH, S, HD], BF16)
    ctxA = scratch("ctxA", [NC, 128, T], BF16)
    x1_tok = scratch("x1_tok", [T, D], F32)
    f_inT = scratch("f_inT", [NC, 128, T], F32)
    hT = scratch("hT", [NC, 128, T], BF16)
    qF = scratch("qF", [NC, 128, T], BF16)
    kF = scratch("kF", [NC, 128, T], BF16)
    vF = scratch("vF", [SEQ_PC, NH, S, HD], BF16)
    ctxF = scratch("ctxF", [NC, 128, T], BF16)
    x1fT = scratch("x1fT", [NC, 128, T], F32)
    h2T = scratch("h2T", [NC, 128, T], BF16)
    combT = scratch("combT", [NE, T], F32)
    sig_eg_d = scratch("sig_eg", [1, T], F32)
    midT = scratch("midT", [NE, NMC, 128, T], BF16)
    smidT = scratch("smidT", [NSC, 128, T], BF16)
    y2T = scratch("y2T", [NC, 128, T], F32)
    wdn_bf = scratch("wdn_bf", [NE * NMC + NSC, 128, D], BF16)
    splat_scr = nc.dram_tensor("splat_scr", [5, 128], F32, kind="Internal").ap()

    with ExitStack() as stack:
        b = Bld(nc, stack)
        sb = lambda name, shape, dt: stack.enter_context(nc.sbuf_tensor(name, shape, dt))[:]

        ABF = Carver(sb("ABF", [128, 31232], BF16), 31232)
        ABF2 = Carver(sb("ABF2", [128, 32768], BF16), 32768)
        AF32 = Carver(sb("AF32", [128, 12288], F32), 12288)
        AF32X = Carver(sb("AF32X", [128, 5248], F32), 5248)

        P0 = stack.enter_context(nc.psum_tensor("P0", [128, 2048], F32))[:]
        P1 = stack.enter_context(nc.psum_tensor("P1", [128, 2048], F32))[:]
        PSAP = [P0[:, 512 * i:512 * (i + 1)] for i in range(4)] + \
               [P1[:, 512 * i:512 * (i + 1)] for i in range(4)]
        PSB = [Buf() for _ in range(8)]

        ident = sb("ident", [128, 128], BF16)
        identf = sb("identf", [128, 128], F32)
        onesb = sb("onesb", [128, 16], BF16)
        onesf = sb("onesf", [128, 16], F32)
        ones_row_bf = sb("ones_row_bf", [1, 128], BF16)
        ones_row_f = sb("ones_row_f", [1, 128], F32)
        repf = sb("repf", [16, 128], F32)
        maskt = sb("maskt", [128, 128], F32)
        ropec = sb("ropec", [128, S], BF16)
        ropes = sb("ropes", [128, S], BF16)
        lnw1 = sb("lnw1", [128, 16], F32)
        lnw2 = sb("lnw2", [128, 16], F32)
        bqf = sb("bqf", [128, 16], F32)
        bkf = sb("bkf", [128, 16], F32)
        wrt = sb("wrt", [128, 16, 8], F32)
        weg = sb("weg", [128, 16], BF16)
        PT = sb("PT", [128, 128], BF16)
        DT = sb("DT", [128, 128], BF16)
        possq = sb("possq", [128, 1], F32)
        pdall = sb("pdall", [128, 1], F32)
        ni2 = sb("ni2", [128, 1], F32)
        ampall = sb("ampall", [128, 1], F32)
        dsb = sb("dsb", [128, 1], F32)
        omdsb = sb("omdsb", [128, 1], F32)
        blendb = sb("blendb", [128, 1], F32)
        omblendb = sb("omblendb", [128, 1], F32)
        cbias = sb("cbias", [128, 8], F32)
        possq8 = sb("possq8", [8, 16], F32)
        pd8 = sb("pd8", [8, 16], F32)
        ni8 = sb("ni8", [8, 16], F32)
        amp8 = sb("amp8", [8, 16], F32)

        bufs = {}

        def B_(name):
            if name not in bufs:
                bufs[name] = Buf()
            return bufs[name]

        def setup_phase():
            b.do("v", lambda h: h.memset(cbias[:, 0:1], EPS_LN), writes=[B_("cbias")])
            b.do("v", lambda h: h.memset(cbias[:, 1:2], EPS_RMS), writes=[B_("cbias")])
            loads = [(ident, id_bf_d, "ident"), (identf, id_f32_d, "identf"),
                     (onesb, ones_bf_d, "onesb"), (onesf, ones_f32_d, "onesf"),
                     (ones_row_bf, ones_row_bf_d, "orb"), (ones_row_f, ones_row_f32_d, "orf"),
                     (repf, rep_d, "repf"),
                     (maskt, mask_d, "maskt"), (ropec, ropecos_d, "ropec"),
                     (ropes, ropesin_d, "ropes"), (lnw1, w_ln1, "lnw1"),
                     (lnw2, w_ln2, "lnw2"), (bqf, bq_f, "bqf"), (bkf, bk_f, "bkf")]
            for dst, src, nm in loads:
                b.dma("y", dst[:], src, writes=[B_(nm)])
            b.dma_group("y", [(wrt[:, c, :], w_router[128 * c:128 * (c + 1), :]) for c in range(NC)],
                        writes=[B_("wrt")])
            b.dma("g", weg[:], w_sh_eg, writes=[B_("weg")])

            al = AF32
            al.reset()
            praw = al.take([128, 128]); draw_ = al.take([128, 128])
            tmp = al.take([128, 128]); stat = al.take([128, 8])
            b.dma("y", praw, sp_pos, writes=[B_("praw")])
            b.dma("y", draw_, sp_dir, writes=[B_("draw")])
            b.dma("y", stat[:, 0:1], sp_lsc, writes=[B_("stat")])

            def _normalize(buf_ap, bufB, extra_scale):
                b.do("s", lambda h: h.activation(tmp, buf_ap, AF.Square), reads=[bufB], writes=[B_("ptmp")])
                b.do("v", lambda h: h.reduce_sum(stat[:, 1:2], tmp, axis=AX.X), reads=[B_("ptmp")], writes=[B_("pn")])
                b.do("s", lambda h: h.activation(stat[:, 2:3], stat[:, 1:2], AF.Sqrt), reads=[B_("pn")], writes=[B_("pn2")])
                b.do("v", lambda h: h.tensor_scalar(stat[:, 2:3], stat[:, 2:3], 1e-12, None, ALU.add),
                     reads=[B_("pn2")], writes=[B_("pn2")])
                b.do("v", lambda h: h.reciprocal(stat[:, 3:4], stat[:, 2:3]), reads=[B_("pn2")], writes=[B_("pn3")])
                if extra_scale is None:
                    b.do("v", lambda h: h.tensor_scalar(buf_ap, buf_ap, stat[:, 3:4], None, ALU.mult),
                         reads=[B_("pn3")], writes=[bufB])
                else:
                    b.do("v", lambda h: h.tensor_scalar(buf_ap, buf_ap, stat[:, 3:4], extra_scale,
                                                        ALU.mult, ALU.mult),
                         reads=[B_("pn3")], writes=[bufB])

            _normalize(praw, B_("praw"), float(np.sqrt(HD) * 0.3))
            _normalize(draw_, B_("draw"), None)
            b.do("s", lambda h: h.activation(tmp, praw, AF.Square), reads=[B_("praw")], writes=[B_("ptmp")])
            b.do("v", lambda h: h.reduce_sum(possq[:], tmp, axis=AX.X), reads=[B_("ptmp")], writes=[B_("possq")])
            b.do("v", lambda h: h.tensor_mul(tmp, praw, draw_), reads=[B_("praw"), B_("draw")], writes=[B_("ptmp")])
            b.do("v", lambda h: h.reduce_sum(pdall[:], tmp, axis=AX.X), reads=[B_("ptmp")], writes=[B_("pdall")])
            b.do("s", lambda h: h.activation(stat[:, 4:5], stat[:, 0:1], AF.Exp), reads=[B_("stat")], writes=[B_("sc1")])
            b.do("v", lambda h: h.tensor_scalar(stat[:, 4:5], stat[:, 4:5], 0.3, 1.2, ALU.max, ALU.min),
                 reads=[B_("sc1")], writes=[B_("sc1")])
            b.do("s", lambda h: h.activation(stat[:, 5:6], stat[:, 4:5], AF.Square), reads=[B_("sc1")], writes=[B_("sc2")])
            b.do("v", lambda h: h.reciprocal(stat[:, 6:7], stat[:, 5:6]), reads=[B_("sc2")], writes=[B_("sc3")])
            b.do("v", lambda h: h.tensor_scalar(ni2[:], stat[:, 6:7], -0.5, None, ALU.mult),
                 reads=[B_("sc3")], writes=[B_("ni2")])
            amp16 = al.take([16, 8]); amps = al.take([16, 4])
            b.dma("y", amp16, sp_lam, writes=[B_("amp16")])
            b.do("v", lambda h: h.reduce_max(amps[:, 0:1], amp16, axis=AX.X, negate=True),
                 reads=[B_("amp16")], writes=[B_("ampm")])
            b.do("s", lambda h: h.activation(amp16, amp16, AF.Exp, bias=amps[:, 0:1], accum_out=amps[:, 1:2]),
                 reads=[B_("amp16"), B_("ampm")], writes=[B_("amp16"), B_("ampsum")])
            b.do("v", lambda h: h.reciprocal(amps[:, 2:3], amps[:, 1:2]), reads=[B_("ampsum")], writes=[B_("amprec")])
            b.do("v", lambda h: h.tensor_scalar(amp16, amp16, amps[:, 2:3], None, ALU.mult),
                 reads=[B_("amp16"), B_("amprec")], writes=[B_("amp16")])
            b.dma("y", splat_scr[4:5, :].rearrange("a (h n) -> (a h) n", n=8), amp16,
                  reads=[B_("amp16")], writes=[B_("sd4")])
            b.dma("y", ampall[:], splat_scr[4:5, :].rearrange("a (b c) -> (a b) c", c=1),
                  reads=[B_("sd4")], writes=[B_("ampall")])
            sds = al.take([1, 4])
            b.dma("y", sds[0:1, 0:1], sp_ds, writes=[B_("sds")])
            b.dma("y", sds[0:1, 1:2], sp_gsa, writes=[B_("sds")])
            b.do("s", lambda h: h.activation(sds[0:1, 0:2], sds[0:1, 0:2], AF.Sigmoid),
                 reads=[B_("sds")], writes=[B_("sds")])
            b.do("v", lambda h: h.tensor_scalar(sds[0:1, 1:2], sds[0:1, 1:2], 0.1, 0.05, ALU.mult, ALU.min),
                 reads=[B_("sds")], writes=[B_("sds")])
            b.mmgroup([lambda h: h.matmul(P0[:, 0:2], ones_row_f[0:1, :], sds[0:1, 0:2],
                                          start=True, stop=True)],
                      reads=[B_("sds"), B_("orf")], writes=[PSB[0]])
            b.do("v", lambda h: h.tensor_copy(dsb[:], P0[:, 0:1]), reads=[PSB[0]], writes=[B_("dsb")])
            b.do("v", lambda h: h.tensor_copy(blendb[:], P0[:, 1:2]), reads=[PSB[0]], writes=[B_("blendb")])
            b.do("v", lambda h: h.tensor_scalar(omdsb[:], dsb[:], -1.0, 1.0, ALU.mult, ALU.add),
                 reads=[B_("dsb")], writes=[B_("omdsb")])
            b.do("v", lambda h: h.tensor_scalar(omblendb[:], blendb[:], -1.0, 1.0, ALU.mult, ALU.add),
                 reads=[B_("blendb")], writes=[B_("omblendb")])
            b.mmgroup([lambda h: h.matmul(P1[:, 0:128], praw, identf, start=True, stop=True)],
                      reads=[B_("praw"), B_("identf")], writes=[PSB[4]])
            b.do("v", lambda h: h.tensor_copy(PT[:], P1[:, 0:128]), reads=[PSB[4]], writes=[B_("PT")])
            b.mmgroup([lambda h: h.matmul(P1[:, 512:640], draw_, identf, start=True, stop=True)],
                      reads=[B_("draw"), B_("identf")], writes=[PSB[5]])
            b.do("v", lambda h: h.tensor_copy(DT[:], P1[:, 512:640]), reads=[PSB[5]], writes=[B_("DT")])
            for i, (src, dst, bn) in enumerate(((possq, possq8, "possq"), (pdall, pd8, "pdall"),
                                                (ni2, ni8, "ni2"), (ampall, amp8, "ampall"))):
                b.dma("y", splat_scr[i:i + 1, :].rearrange("a (b c) -> (a b) c", c=1), src[:],
                      reads=[B_(bn)], writes=[B_("sd%d" % i)])
                b.dma_group("y", [(dst[:, hh:hh + 1],
                                   splat_scr[i:i + 1, 8 * hh:8 * (hh + 1)].rearrange("a (b c) -> (a b) c", c=1))
                                  for hh in range(NH)],
                            reads=[B_("sd%d" % i)], writes=[B_(bn + "8")])

        def phase_ln0():
            al = AF32
            al.reset()
            xt = [al.take([128, 2048]) for _ in range(2)]
            sq = al.take([128, 2048])
            st_ = al.take([128, 8])
            xtB = [B_("p0x%d" % i) for i in range(2)]
            sqB = B_("p0sq"); stB = B_("p0st")
            ABF.reset()
            lnb = [ABF.take([128, 2048]) for _ in range(4)]
            lnbB = [B_("p0l%d" % i) for i in range(4)]
            evb = [ABF.take([128, 512]) for _ in range(4)]
            evbB = [B_("p0e%d" % i) for i in range(4)]
            for g in range(NTT):
                for j in range(4):
                    tb = 4 * g + j
                    xb = xt[tb % 2]; xbB = xtB[tb % 2]
                    b.dma("y", xb, x_in[128 * tb:128 * (tb + 1), :], writes=[xbB])
                    b.do("v", lambda h, xb=xb: h.reduce_sum(st_[:, 0:1], xb, axis=AX.X),
                         reads=[xbB], writes=[stB])
                    b.do("s", lambda h, xb=xb: h.activation(sq, xb, AF.Square),
                         reads=[xbB], writes=[sqB])
                    b.do("v", lambda h: h.reduce_sum(st_[:, 1:2], sq, axis=AX.X),
                         reads=[sqB], writes=[stB])
                    b.do("v", lambda h: h.tensor_scalar(st_[:, 2:3], st_[:, 0:1], 1.0 / D, None, ALU.mult),
                         reads=[stB], writes=[stB])
                    b.do("v", lambda h: h.tensor_mul(st_[:, 3:4], st_[:, 2:3], st_[:, 2:3]),
                         reads=[stB], writes=[stB])
                    b.do("v", lambda h: h.scalar_tensor_tensor(st_[:, 4:5], st_[:, 1:2], 1.0 / D,
                                                               st_[:, 3:4], ALU.mult, ALU.subtract),
                         reads=[stB], writes=[stB])
                    b.do("s", lambda h: h.activation(st_[:, 5:6], st_[:, 4:5], AF.Sqrt, bias=cbias[:, 0:1]),
                         reads=[stB, B_("cbias")], writes=[stB])
                    b.do("v", lambda h: h.reciprocal(st_[:, 6:7], st_[:, 5:6]),
                         reads=[stB], writes=[stB])
                    b.do("v", lambda h, xb=xb, o=lnb[j]: h.tensor_scalar(
                            o, xb, st_[:, 2:3], st_[:, 6:7], ALU.subtract, ALU.mult),
                         reads=[xbB, stB], writes=[lnbB[j]])
                for c in range(NC):
                    bank = c % 4
                    fns = [(lambda h, j=j, c=c, bank=bank:
                            h.matmul(PSAP[bank][:, 128 * j:128 * (j + 1)],
                                     lnb[j][:, 128 * c:128 * (c + 1)], ident,
                                     start=True, stop=True)) for j in range(4)]
                    b.mmgroup(fns, reads=[lnbB[0], lnbB[1], lnbB[2], lnbB[3], B_("ident")],
                              writes=[PSB[bank]])
                    eb = evb[c % 4]; ebB = evbB[c % 4]
                    if c % 2 == 0:
                        b.do("v", lambda h, eb=eb, bank=bank: h.tensor_copy(eb, PSAP[bank]),
                             reads=[PSB[bank]], writes=[ebB])
                    else:
                        b.do("s", lambda h, eb=eb, bank=bank: h.activation(eb, PSAP[bank], AF.Copy),
                             reads=[PSB[bank]], writes=[ebB])
                    b.dma("s", ln0T[c, :, 512 * g:512 * (g + 1)], eb, reads=[ebB])
            b.barrier()

        def wload(wbuf, wbufB, rows_of):
            b.dma_group("g", [(wbuf[:, k, :], rows_of(k)) for k in range(NC)],
                        writes=[wbufB])

        def gemm_fm(src_scr, wbuf, wbufB, Mn, evac_cb, xb2, xb2B, width=TTW, xq="s"):
            ntt = T // width
            for tt in range(ntt):
                xb = xb2[tt % 2]; xbB = xb2B[tt % 2]
                b.dma_group(xq, [(xb[:, k, :], src_scr[k, :, width * tt:width * (tt + 1)])
                                 for k in range(NC)], writes=[xbB])
                for m in range(Mn):
                    bank = m % 4
                    fns = [(lambda h, k=k, m=m, bank=bank, xb=xb:
                            h.matmul(PSAP[bank][:, 0:width], wbuf[:, k, 128 * m:128 * (m + 1)],
                                     xb[:, k, :], start=(k == 0), stop=(k == NC - 1)))
                           for k in range(NC)]
                    b.mmgroup(fns, reads=[wbufB, xbB], writes=[PSB[bank]])
                    evac_cb(tt, m, bank)

        def mk_evac_bf16(dst_scr, ev4, ev4B, width=TTW, qdma="y"):
            def cb(tt, m, bank):
                eb = ev4[m % 4]; ebB = ev4B[m % 4]
                if m % 2 == 0:
                    b.do("v", lambda h, eb=eb, bank=bank: h.tensor_copy(eb, PSAP[bank][:, 0:width]),
                         reads=[PSB[bank]], writes=[ebB])
                else:
                    b.do("s", lambda h, eb=eb, bank=bank: h.activation(eb, PSAP[bank][:, 0:width], AF.Copy),
                         reads=[PSB[bank]], writes=[ebB])
                b.dma(qdma, dst_scr[m, :, width * tt:width * (tt + 1)], eb, reads=[ebB])
            return cb

        def phase_qkv(block):
            a = (block == "A")
            src = ln0T if a else hT
            wq_, wk_, wv_ = (wq_a, wk_a, wv_a) if a else (wq_f, wk_f, wv_f)
            qdst, kdst, vdst = (qA, kA, vA) if a else (qF, kF, vF)
            ABF.reset()
            xb2 = [ABF.take([128, NC, TTW]) for _ in range(2)]
            xb2B = [B_("qkvx%d" % i) for i in range(2)]
            ev4 = [ABF.take([128, TTW]) for _ in range(4)]
            ev4B = [B_("qkve%d" % i) for i in range(4)]
            ABF2.reset()
            wbuf = ABF2.take([128, NC, D])
            wbufB = B_("qkvw")
            AF32.reset()
            evf = [AF32.take([128, TTW]) for _ in range(2)]
            evfB = [B_("qkvf%d" % i) for i in range(2)]
            rot = AF32.take([128, TTW])
            rotB = B_("qkvrot")
            bvb = AF32.take([128, D]) if not a else None
            bvrow = AF32.take([1, D]) if not a else None

            if not a:
                b.dma("y", bvrow, bv_f, writes=[B_("bvrow")])
                for n in range(4):
                    b.mmgroup([lambda h, n=n: h.matmul(P1[:, 512 * n:512 * (n + 1)],
                                                       ones_row_f[0:1, :], bvrow[0:1, 512 * n:512 * (n + 1)],
                                                       start=True, stop=True)],
                              reads=[B_("bvrow"), B_("orf")], writes=[PSB[4 + n]])
                    b.do("v", lambda h, n=n: h.tensor_copy(bvb[:, 512 * n:512 * (n + 1)],
                                                           P1[:, 512 * n:512 * (n + 1)]),
                         reads=[PSB[4 + n]], writes=[B_("bvb")])

            for which, wsrc, dst, bias_ap, biasB in (("q", wq_, qdst, (None if a else bqf), "bqf"),
                                                     ("k", wk_, kdst, (None if a else bkf), "bkf")):
                wload(wbuf, wbufB, lambda k, wsrc=wsrc: wsrc[128 * k:128 * (k + 1), :])

                if a:
                    cb = mk_evac_bf16(dst, ev4, ev4B)
                else:
                    def cb(tt, m, bank, dst=dst, bias_ap=bias_ap, biasB=biasB):
                        pos0 = (tt % 4) * TTW
                        ef = evf[m % 2]; efB = evfB[m % 2]
                        eb = ev4[m % 4]; ebB = ev4B[m % 4]
                        b.do("s", lambda h, ef=ef, bank=bank, m=m:
                             h.activation(ef, PSAP[bank], AF.Identity, bias=bias_ap[:, m:m + 1]),
                             reads=[PSB[bank], B_(biasB)], writes=[efB])
                        b.do("v", lambda h, ef=ef: h.tensor_scalar(rot[0:64, :], ef[64:128, :],
                                                                   -1.0, None, ALU.mult),
                             reads=[efB], writes=[rotB])
                        b.do("v", lambda h, ef=ef: h.tensor_copy(rot[64:128, :], ef[0:64, :]),
                             reads=[efB], writes=[rotB])
                        b.do("v", lambda h, ef=ef, pos0=pos0:
                             h.tensor_mul(ef, ef, ropec[:, pos0:pos0 + TTW]),
                             reads=[efB, B_("ropec")], writes=[efB])
                        b.do("v", lambda h, pos0=pos0:
                             h.tensor_mul(rot, rot, ropes[:, pos0:pos0 + TTW]),
                             reads=[rotB, B_("ropes")], writes=[rotB])
                        b.do("v", lambda h, ef=ef, eb=eb: h.tensor_add(eb, ef, rot),
                             reads=[efB, rotB], writes=[ebB])
                        b.dma("y", dst[m, :, TTW * tt:TTW * (tt + 1)], eb, reads=[ebB])
                gemm_fm(src, wbuf, wbufB, NC, cb, xb2, xb2B, xq="s")

            wload(wbuf, wbufB, lambda k: wv_[128 * k:128 * (k + 1), :])
            for tt in range(NTT):
                xb = xb2[tt % 2]; xbB = xb2B[tt % 2]
                b.dma_group("s", [(xb[:, k, :], src[k, :, TTW * tt:TTW * (tt + 1)])
                                  for k in range(NC)], writes=[xbB])
                sidx = tt // 4
                for tb in range(4):
                    t0 = (tt % 4) * TTW + tb * 128
                    for n in range(4):
                        bank = (tb * 4 + n) % 4
                        fns = [(lambda h, k=k, tb=tb, n=n, bank=bank, xb=xb:
                                h.matmul(PSAP[bank], xb[:, k, 128 * tb:128 * (tb + 1)],
                                         wbuf[:, k, 512 * n:512 * (n + 1)],
                                         start=(k == 0), stop=(k == NC - 1)))
                               for k in range(NC)]
                        b.mmgroup(fns, reads=[wbufB, xbB], writes=[PSB[bank]])
                        eb = ev4[bank]; ebB = ev4B[bank]
                        if a:
                            b.do("v", lambda h, eb=eb, bank=bank: h.tensor_copy(eb, PSAP[bank]),
                                 reads=[PSB[bank]], writes=[ebB])
                        else:
                            b.do("v", lambda h, eb=eb, bank=bank, n=n:
                                 h.tensor_add(eb, PSAP[bank], bvb[:, 512 * n:512 * (n + 1)]),
                                 reads=[PSB[bank], B_("bvb")], writes=[ebB])
                        b.dma_group("y", [(vdst[sidx, 4 * n + j, t0:t0 + 128, :],
                                           eb[:, 128 * j:128 * (j + 1)]) for j in range(4)],
                                    reads=[ebB])
            b.barrier()

        def phase_attn(block):
            a = (block == "A")
            qsrc, ksrc, vsrc, cdst = (qA, kA, vA, ctxA) if a else (qF, kF, vF, ctxF)
            ABF.reset()
            qh2 = [ABF.take([128, S]) for _ in range(2)]
            kh2 = [ABF.take([128, S]) for _ in range(2)]
            vh2 = [ABF.take([128, NC, 129]) for _ in range(2)]
            ats2 = [ABF.take([128, S]) for _ in range(2)]
            ctxb = ABF.take([128, S])
            qh2B = [B_("ath_q%d" % i) for i in range(2)]
            kh2B = [B_("ath_k%d" % i) for i in range(2)]
            vh2B = [B_("ath_v%d" % i) for i in range(2)]
            ats2B = [B_("ath_t%d" % i) for i in range(2)]
            ctxbB = B_("ath_ctx")
            ABF2.reset()
            attn = ABF2.take([128, NC, S])
            attnB = B_("ath_attn")
            AF32.reset()
            SA = AF32.take([128, S]); SB2 = AF32.take([128, S])
            SC2 = AF32.take([128, S]); SD = AF32.take([128, S])
            tok16 = AF32.take([16, S])
            SAB = B_("ath_SA"); SB2B = B_("ath_SB2"); SC2B = B_("ath_SC2")
            SDB = B_("ath_SD"); tok16B = B_("ath_tok16")
            AF32X.reset()
            stt_ = AF32X.take([128, 64])
            sttB = B_("ath_stt")
            nmx = stt_[:, 0:16]; nms = stt_[:, 16:32]; rs_ = stt_[:, 32:48]; rr_ = stt_[:, 48:64]
            if a:
                qaP3 = ABF.take([128, S]); qaP3B = B_("ath_qaP3")
                kaP2 = ctxb; kaP2B = ctxbB
                qah2 = [ABF.take([128, S]) for _ in range(2)]
                qah2B = [B_("ath_qah%d" % i) for i in range(2)]
                katokAll = ABF.take([128, NH, NC, 8]); katokAllB = B_("ath_katok")
                kvall = ABF.take([8, NH, 129]); kvallB = B_("ath_kvall")
                ksbd = ABF.take([128, 16]); ksbdB = B_("ath_ksbd")
                BDp = qah2[0].rearrange("p (a c) -> p a c", a=NH, c=128); BDpB = qah2B[0]
                BDd = qah2[1].rearrange("p (a c) -> p a c", a=NH, c=128); BDdB = qah2B[1]
                idcol = ABF.take([128, 16, 16]); idcolB = B_("ath_idcol")
                b.do("v", lambda h: h.memset(idcol.rearrange("p a b -> p (a b)"), 0.0), writes=[idcolB])
                for hh in range(NH):
                    b.do("v", lambda h, hh=hh: h.memset(idcol[:, hh, hh:hh + 1], 1.0), writes=[idcolB])
                for i in range(2):
                    b.do("v", lambda h, i=i: h.memset(vh2[i][:, :, 128:129], 1.0), writes=[vh2B[i]])

            def load_head(s, hh, par, with_v=True, with_k=True):
                b.dma("y", qh2[par], qsrc[hh, :, S * s:S * (s + 1)], writes=[qh2B[par]])
                if with_k:
                    b.dma("s", kh2[par], ksrc[hh, :, S * s:S * (s + 1)], writes=[kh2B[par]])
                if with_v:
                    b.dma_group("g", [(vh2[par][:, c, 0:128], vsrc[s, hh, 128 * c:128 * (c + 1), :])
                                      for c in range(NC)], writes=[vh2B[par]])

            def splat_pass(s):
                qsq = ats2[0]; qsqB = ats2B[0]
                b.do("v", lambda h: h.memset(BDp.rearrange("p a b -> p (a b)"), 0.0), writes=[BDpB])
                b.do("v", lambda h: h.memset(BDd.rearrange("p a b -> p (a b)"), 0.0), writes=[BDdB])
                for hh in range(NH):
                    b.do("v", lambda h, hh=hh: h.tensor_copy(BDp[:, hh, 8 * hh:8 * hh + 8],
                                                             PT[:, 8 * hh:8 * hh + 8]),
                         reads=[B_("PT")], writes=[BDpB])
                    b.do("v", lambda h, hh=hh: h.tensor_copy(BDd[:, hh, 8 * hh:8 * hh + 8],
                                                             DT[:, 8 * hh:8 * hh + 8]),
                         reads=[B_("DT")], writes=[BDdB])
                for which in ("q", "k"):
                    srcs = qsrc if which == "q" else ksrc
                    out_bf, out_bfB = (qaP3, qaP3B) if which == "q" else (kaP2, kaP2B)
                    for half in range(2):
                        q0 = 1024 * half
                        for hh in range(NH):
                            par = hh % 2
                            b.dma("y", qh2[par][:, 0:1024], srcs[hh, :, S * s + q0:S * s + q0 + 1024],
                                  writes=[qh2B[par]])
                            xap = qh2[par][:, 0:1024]
                            b.do("s", lambda h, xap=xap, hh=hh: h.activation(
                                    qsq[:, 0:1024], xap, AF.Square),
                                 reads=[qh2B[par]], writes=[qsqB])
                            fns = [(lambda h, j=j, hh=hh, xap=xap: h.matmul(
                                    P0[:, 512 * j:512 * (j + 1)], BDp[:, hh, :],
                                    xap[:, 512 * j:512 * (j + 1)],
                                    start=(hh == 0), stop=(hh == NH - 1))) for j in range(2)]
                            b.mmgroup(fns, reads=[qh2B[par], BDpB],
                                      war=(PSB[0:2] if hh == 0 else ()),
                                      writes=(PSB[0:2] if hh == NH - 1 else ()))
                            fns = [(lambda h, j=j, hh=hh, xap=xap: h.matmul(
                                    P0[:, 1024 + 512 * j:1024 + 512 * (j + 1)], BDd[:, hh, :],
                                    xap[:, 512 * j:512 * (j + 1)],
                                    start=(hh == 0), stop=(hh == NH - 1))) for j in range(2)]
                            b.mmgroup(fns, reads=[qh2B[par], BDdB],
                                      war=(PSB[2:4] if hh == 0 else ()),
                                      writes=(PSB[2:4] if hh == NH - 1 else ()))
                            fns = [(lambda h, j=j, hh=hh: h.matmul(
                                    P1[0:16, 512 * j:512 * (j + 1)], idcol[:, hh, :],
                                    qsq[:, 512 * j:512 * (j + 1)],
                                    start=(hh == 0), stop=(hh == NH - 1))) for j in range(2)]
                            b.mmgroup(fns, reads=[qsqB, idcolB],
                                      war=(PSB[4:6] if hh == 0 else ()),
                                      writes=(PSB[4:6] if hh == NH - 1 else ()))
                        b.do("v", lambda h, q0=q0: h.tensor_copy(SA[:, q0:q0 + 1024], P0[:, 0:1024]),
                             reads=PSB[0:2], writes=[SAB])
                        b.do("v", lambda h, q0=q0: h.tensor_copy(SB2[:, q0:q0 + 1024], P0[:, 1024:2048]),
                             reads=PSB[2:4], writes=[SB2B])
                        b.do("v", lambda h, q0=q0: h.tensor_copy(tok16[:, q0:q0 + 1024], P1[0:16, 0:1024]),
                             reads=PSB[4:6], writes=[tok16B])
                        fns = [(lambda h, j=j, q0=q0: h.matmul(
                                P1[:, 1024 + 512 * j:1024 + 512 * (j + 1)], repf,
                                tok16[:, q0 + 512 * j:q0 + 512 * (j + 1)],
                                start=True, stop=True)) for j in range(2)]
                        b.mmgroup(fns, reads=[tok16B, B_("repf")], writes=PSB[6:8])
                        b.do("v", lambda h, q0=q0: h.scalar_tensor_tensor(
                                SA[:, q0:q0 + 1024], SA[:, q0:q0 + 1024], -2.0,
                                P1[:, 1024:2048], ALU.mult, ALU.add),
                             reads=[SAB] + PSB[6:8], writes=[SAB])
                    b.do("v", lambda h: h.tensor_scalar(SA, SA, possq[:, 0:1], 0.0, ALU.add, ALU.max),
                         reads=[SAB, B_("possq")], writes=[SAB])
                    b.do("s", lambda h: h.activation(SC2, SA, AF.Exp, scale=ni2[:, 0:1]),
                         reads=[SAB, B_("ni2")], writes=[SC2B])
                    b.do("v", lambda h: h.tensor_scalar(SB2, SB2, pdall[:, 0:1], None, ALU.subtract),
                         reads=[SB2B, B_("pdall")], writes=[SB2B])
                    b.do("v", lambda h: h.tensor_mul(SD, SB2, SB2), reads=[SB2B], writes=[SDB])
                    b.do("v", lambda h: h.tensor_sub(SD, SA, SD), reads=[SAB, SDB], writes=[SDB])
                    b.do("v", lambda h: h.tensor_scalar(SD, SD, 0.0, None, ALU.max),
                         reads=[SDB], writes=[SDB])
                    b.do("s", lambda h: h.activation(SB2, SD, AF.Exp, scale=ni2[:, 0:1]),
                         reads=[SDB, B_("ni2")], writes=[SB2B])
                    b.do("v", lambda h: h.tensor_scalar(SB2, SB2, dsb[:, 0:1], None, ALU.mult),
                         reads=[SB2B, B_("dsb")], writes=[SB2B])
                    b.do("v", lambda h: h.scalar_tensor_tensor(SC2, SC2, omdsb[:, 0:1], SB2,
                                                               ALU.mult, ALU.add),
                         reads=[SC2B, SB2B, B_("omdsb")], writes=[SC2B])
                    if which == "q":
                        b.do("v", lambda h: h.tensor_scalar(qaP3, SC2, ampall[:, 0:1], None, ALU.mult),
                             reads=[SC2B, B_("ampall")], writes=[qaP3B])
                    else:
                        b.do("v", lambda h: h.tensor_copy(kaP2, SC2), reads=[SC2B], writes=[kaP2B])
                for hh in range(NH):
                    par = hh % 2
                    kah = qah2[par][0:8, :]; kahB = qah2B[par]
                    b.dma("y", kah, kaP2[8 * hh:8 * hh + 8, :], reads=[kaP2B], writes=[kahB])
                    fns = [(lambda h, c=c: h.matmul(P0[:, 1536 + 8 * c:1544 + 8 * c],
                                                    kah[:, 128 * c:128 * (c + 1)], ident[0:8, 0:8],
                                                    start=True, stop=True)) for c in range(NC)]
                    b.mmgroup(fns, reads=[kahB, B_("ident")], writes=[PSB[3]])
                    b.do("v", lambda h, hh=hh: h.tensor_copy(
                            katokAll[:, hh, :, :].rearrange("p a b -> p (a b)"), P0[:, 1536:1664]),
                         reads=[PSB[3]], writes=[katokAllB])
                    load_head(s, hh, par, with_k=False)
                    fns = [(lambda h, c=c, hh=hh, par=par: h.matmul(
                            P1[0:8, 0:129], katokAll[:, hh, c, :], vh2[par][:, c, :],
                            start=(c == 0), stop=(c == NC - 1))) for c in range(NC)]
                    b.mmgroup(fns, reads=[katokAllB, vh2B[par]], writes=[PSB[4]])
                    b.do("v", lambda h, hh=hh: h.tensor_copy(kvall[:, hh, :], P1[0:8, 0:129]),
                         reads=[PSB[4]], writes=[kvallB])
                b.do("v", lambda h: h.memset(ksbd, 0.0), writes=[ksbdB])
                b.dma_group("y", [(ksbd[8 * hh:8 * hh + 8, hh:hh + 1], kvall[:, hh, 128:129])
                                  for hh in range(NH)], reads=[kvallB], writes=[ksbdB])
                fns = [(lambda h, j=j: h.matmul(P1[0:16, 512 * j:512 * (j + 1)], ksbd,
                                                qaP3[:, 512 * j:512 * (j + 1)],
                                                start=True, stop=True)) for j in range(4)]
                b.mmgroup(fns, reads=[ksbdB, qaP3B], writes=PSB[4:8])
                b.do("v", lambda h: h.tensor_scalar(tok16, P1[0:16, :], 1e-8, None, ALU.add),
                     reads=PSB[4:8], writes=[tok16B])
                b.do("v", lambda h: h.reciprocal(tok16, tok16), reads=[tok16B], writes=[tok16B])
                fns = [(lambda h, j=j: h.matmul(P1[:, 512 * j:512 * (j + 1)], repf,
                                                tok16[:, 512 * j:512 * (j + 1)],
                                                start=True, stop=True)) for j in range(4)]
                b.mmgroup(fns, reads=[tok16B, B_("repf")], writes=PSB[4:8])
                b.do("v", lambda h: h.tensor_tensor(qaP3, qaP3, P1[:, :], ALU.mult),
                     reads=[qaP3B] + PSB[4:8], writes=[qaP3B])

            for s in range(SEQ_PC):
                if a:
                    splat_pass(s)
                load_head(s, 0, 0)
                for hh in range(NH):
                    par = hh % 2
                    if hh + 1 < NH:
                        load_head(s, hh + 1, 1 - par)
                    elif a and s + 1 < SEQ_PC:
                        pass
                    elif not a and s + 1 < SEQ_PC:
                        load_head(s + 1, 0, 1 - par)
                    qh, qhB = qh2[par], qh2B[par]
                    kh, khB = kh2[par], kh2B[par]
                    vh, vhB = vh2[par], vh2B[par]
                    if a:
                        qah = qah2[par][0:8, :]; qahB = qah2B[par]
                        b.dma("y", qah, qaP3[8 * hh:8 * hh + 8, :], reads=[qaP3B], writes=[qahB])

                    for qt in range(NC):
                        nk = (qt + 1) * 128 if not a else S
                        pg = [PSB[0], PSB[1], PSB[2], PSB[3]] if qt % 2 == 0 else \
                             [PSB[4], PSB[5], PSB[6], PSB[7]]
                        P_ = P0 if qt % 2 == 0 else P1
                        nb = (nk + 511) // 512
                        fns = []
                        for j in range(nb):
                            w = min(512, nk - 512 * j)
                            fns.append(lambda h, j=j, w=w, qt=qt, P_=P_, qh=qh, kh=kh:
                                       h.matmul(P_[:, 512 * j:512 * j + w],
                                                qh[:, 128 * qt:128 * (qt + 1)],
                                                kh[:, 512 * j:512 * j + w],
                                                start=True, stop=True))
                        b.mmgroup(fns, reads=[qhB, khB], writes=pg[:nb])
                        if not a:
                            b.do("v", lambda h, P_=P_, qt=qt: h.tensor_tensor(
                                    P_[:, 128 * qt:128 * (qt + 1)], P_[:, 128 * qt:128 * (qt + 1)],
                                    maskt, ALU.add),
                                 reads=[B_("maskt")] + pg[qt // 4:qt // 4 + 1], writes=[pg[qt // 4]])
                        b.do("v", lambda h, P_=P_, nk=nk, qt=qt: h.reduce_max(
                                nmx[:, qt:qt + 1], P_[:, 0:nk], axis=AX.X, negate=True),
                             reads=pg[:nb], writes=[sttB])
                        b.do("v", lambda h, qt=qt: h.tensor_scalar(
                                nms[:, qt:qt + 1], nmx[:, qt:qt + 1], INV_SQRT_HD, None, ALU.mult),
                             reads=[sttB], writes=[sttB])
                        b.do("s", lambda h, P_=P_, nk=nk, qt=qt: h.activation(
                                attn[:, qt, 0:nk], P_[:, 0:nk], AF.Exp, scale=INV_SQRT_HD,
                                bias=nms[:, qt:qt + 1], accum_out=rs_[:, qt:qt + 1]),
                             reads=pg[:nb] + [sttB], writes=[attnB, sttB])
                        b.do("v", lambda h, qt=qt: h.reciprocal(rr_[:, qt:qt + 1], rs_[:, qt:qt + 1]),
                             reads=[sttB], writes=[sttB])
                        b.do("v", lambda h, nk=nk, qt=qt: h.tensor_scalar(
                                attn[:, qt, 0:nk], attn[:, qt, 0:nk], rr_[:, qt:qt + 1], None, ALU.mult),
                             reads=[sttB, attnB], writes=[attnB])

                    for kc in range(NC):
                        qt_lo = 0 if a else kc
                        for p in range(qt_lo // 4, 4):
                            bank = p % 2
                            qts = [q_ for q_ in range(max(qt_lo, 4 * p), 4 * p + 4)]
                            fns = [(lambda h, q_=q_, p=p, bank=bank, kc=kc:
                                    h.matmul(P0[:, 512 * bank + 128 * (q_ - 4 * p):512 * bank + 128 * (q_ - 4 * p) + 128],
                                             attn[:, q_, 128 * kc:128 * (kc + 1)], ident,
                                             start=True, stop=True)) for q_ in qts]
                            b.mmgroup(fns, reads=[attnB, B_("ident")], writes=[PSB[bank]])
                            ats = ats2[kc % 2]; atsB = ats2B[kc % 2]
                            q0 = max(qt_lo, 4 * p) * 128
                            w = (4 * p + 4) * 128 - q0
                            off = 512 * bank + (q0 - 512 * p)
                            if p % 2 == 0:
                                b.do("v", lambda h, ats=ats, off=off, w=w, q0=q0:
                                     h.tensor_copy(ats[:, q0:q0 + w], P0[:, off:off + w]),
                                     reads=[PSB[bank]], writes=[atsB])
                            else:
                                b.do("s", lambda h, ats=ats, off=off, w=w, q0=q0:
                                     h.activation(ats[:, q0:q0 + w], P0[:, off:off + w], AF.Copy),
                                     reads=[PSB[bank]], writes=[atsB])
                        ats = ats2[kc % 2]; atsB = ats2B[kc % 2]
                        q0all = qt_lo * 128
                        first_j = q0all // 512
                        fns = []
                        for j in range(first_j, 4):
                            lo = max(q0all, 512 * j)
                            wdt = 512 * (j + 1) - lo
                            fns.append(lambda h, lo=lo, wdt=wdt, kc=kc, vh=vh, ats=ats:
                                       h.matmul(P1[:, lo:lo + wdt], vh[:, kc, 0:128], ats[:, lo:lo + wdt],
                                                start=(kc == 0), stop=(kc == NC - 1)))
                        if kc == 0:
                            b.mmgroup(fns, reads=[atsB, vhB], war=PSB[4:8], writes=[])
                        elif kc == NC - 1:
                            b.mmgroup(fns, reads=[atsB, vhB], writes=PSB[4:8])
                        else:
                            b.mmgroup(fns, reads=[atsB, vhB])

                    if a:
                        for j in range(4):
                            b.do("v", lambda h, j=j: h.tensor_scalar(
                                    ctxb[:, 512 * j:512 * (j + 1)], P1[:, 512 * j:512 * (j + 1)],
                                    omblendb[:, 0:1], None, ALU.mult),
                                 reads=[PSB[4 + j], B_("omblendb")], writes=[ctxbB])
                        fns = [(lambda h, j=j, hh=hh: h.matmul(P0[:, 512 * j:512 * (j + 1)],
                                                               kvall[:, hh, 0:128],
                                                               qah[:, 512 * j:512 * (j + 1)],
                                                               start=True, stop=True)) for j in range(4)]
                        b.mmgroup(fns, reads=[kvallB, qahB], writes=PSB[0:4])
                        for j in range(4):
                            b.do("v", lambda h, j=j: h.scalar_tensor_tensor(
                                    ctxb[:, 512 * j:512 * (j + 1)], P0[:, 512 * j:512 * (j + 1)],
                                    blendb[:, 0:1], ctxb[:, 512 * j:512 * (j + 1)], ALU.mult, ALU.add),
                                 reads=[PSB[j], B_("blendb"), ctxbB], writes=[ctxbB])
                    else:
                        for j in range(4):
                            if j % 2 == 0:
                                b.do("v", lambda h, j=j: h.tensor_copy(
                                        ctxb[:, 512 * j:512 * (j + 1)], P1[:, 512 * j:512 * (j + 1)]),
                                     reads=[PSB[4 + j]], writes=[ctxbB])
                            else:
                                b.do("s", lambda h, j=j: h.activation(
                                        ctxb[:, 512 * j:512 * (j + 1)], P1[:, 512 * j:512 * (j + 1)], AF.Copy),
                                     reads=[PSB[4 + j]], writes=[ctxbB])
                    b.dma("y", cdst[hh, :, S * s:S * (s + 1)], ctxb, reads=[ctxbB])
            b.barrier()

        def phase_p3():
            ABF.reset()
            xb2 = [ABF.take([128, NC, TTW]) for _ in range(2)]
            xb2B = [B_("p3x%d" % i) for i in range(2)]
            hbf2 = [ABF.take([128, D]) for _ in range(2)]
            hbf2B = [B_("p3hbf%d" % i) for i in range(2)]
            hst2 = [ABF.take([128, D]) for _ in range(2)]
            hst2B = [B_("p3hst%d" % i) for i in range(2)]
            ABF2.reset()
            wbuf = ABF2.take([128, NC, D]); wbufB = B_("p3w")
            AF32.reset()
            EB = AF32.take([128, NC, TTW])
            EBB = B_("p3eb")
            xtok2 = [AF32.take([128, D]) for _ in range(2)]
            xtok2B = [B_("p3xt%d" % i) for i in range(2)]
            AF32X.reset()
            st2 = AF32X.take([128, 32]); st2B = [B_("p3st0"), B_("p3st1")]
            f32st = [AF32X.take([128, TTW]) for _ in range(2)]
            f32stB = [B_("p3f%d" % i) for i in range(2)]
            aot2 = [AF32X.take([128, D]) for _ in range(2)]
            aot2B = [B_("p3aot%d" % i) for i in range(2)]

            wload(wbuf, wbufB, lambda k: wo_a[128 * k:128 * (k + 1), :])
            for tt in range(NTT):
                xb = xb2[tt % 2]; xbB = xb2B[tt % 2]
                b.dma_group("s", [(xb[:, k, :], ctxA[k, :, TTW * tt:TTW * (tt + 1)])
                                  for k in range(NC)], writes=[xbB])
                for m in range(NC):
                    bank = m % 4
                    fns = [(lambda h, k=k, m=m, bank=bank, xb=xb:
                            h.matmul(PSAP[bank], wbuf[:, k, 128 * m:128 * (m + 1)], xb[:, k, :],
                                     start=(k == 0), stop=(k == NC - 1))) for k in range(NC)]
                    b.mmgroup(fns, reads=[wbufB, xbB], writes=[PSB[bank]])
                    if m % 2 == 0:
                        b.do("v", lambda h, m=m, bank=bank: h.tensor_copy(EB[:, m, :], PSAP[bank]),
                             reads=[PSB[bank]], writes=[EBB])
                    else:
                        b.do("s", lambda h, m=m, bank=bank: h.activation(EB[:, m, :], PSAP[bank], AF.Copy),
                             reads=[PSB[bank]], writes=[EBB])
                for tb in range(4):
                    t0 = TTW * tt + 128 * tb
                    par = tb % 2
                    aot, aotB = aot2[par], aot2B[par]
                    xt, xtB = xtok2[par], xtok2B[par]
                    st = st2[:, 16 * par:16 * par + 16]
                    stB = st2B[par]
                    hbf, hbfB = hbf2[par], hbf2B[par]
                    hst, hstB = hst2[par], hst2B[par]
                    psg = PSB[4:6] if par == 0 else PSB[6:8]
                    Pq = P1[:, 0:1024] if par == 0 else P1[:, 1024:2048]
                    fns_a = [(lambda h, c=c, tb=tb, Pq=Pq: h.matmul(
                            Pq[:, (c % 8) * 128:(c % 8) * 128 + 128],
                            EB[:, c, 128 * tb:128 * (tb + 1)], identf,
                            start=True, stop=True)) for c in range(8)]
                    b.mmgroup(fns_a, reads=[EBB, B_("identf")], writes=psg)
                    b.do("v", lambda h, aot=aot, Pq=Pq: h.tensor_copy(aot[:, 0:1024], Pq),
                         reads=psg, writes=[aotB])
                    fns_b = [(lambda h, c=c, tb=tb, Pq=Pq: h.matmul(
                            Pq[:, (c % 8) * 128:(c % 8) * 128 + 128],
                            EB[:, c, 128 * tb:128 * (tb + 1)], identf,
                            start=True, stop=True)) for c in range(8, NC)]
                    b.mmgroup(fns_b, reads=[EBB, B_("identf")], writes=psg)
                    b.do("v", lambda h, aot=aot, Pq=Pq: h.tensor_copy(aot[:, 1024:2048], Pq),
                         reads=psg, writes=[aotB])
                    b.dma("y", xt, x_in[t0:t0 + 128, :], writes=[xtB])
                    b.do("v", lambda h, aot=aot, st=st: h.reduce_sum(st[:, 0:1], aot, axis=AX.X),
                         reads=[aotB], writes=[stB])
                    b.do("s", lambda h, aot=aot, par=par: h.activation(f32st[par], aot[:, 0:TTW], AF.Square),
                         reads=[aotB], writes=[f32stB[par]])
                    b.do("v", lambda h, st=st, par=par: h.reduce_sum(st[:, 1:2], f32st[par], axis=AX.X),
                         reads=[f32stB[par]], writes=[stB])
                    b.do("s", lambda h, aot=aot, par=par: h.activation(f32st[par], aot[:, TTW:2 * TTW], AF.Square),
                         reads=[aotB], writes=[f32stB[par]])
                    b.do("v", lambda h, st=st, par=par: h.reduce_sum(st[:, 2:3], f32st[par], axis=AX.X),
                         reads=[f32stB[par]], writes=[stB])
                    b.do("s", lambda h, aot=aot, par=par: h.activation(f32st[par], aot[:, 2 * TTW:3 * TTW], AF.Square),
                         reads=[aotB], writes=[f32stB[par]])
                    b.do("v", lambda h, st=st, par=par: h.reduce_sum(st[:, 3:4], f32st[par], axis=AX.X),
                         reads=[f32stB[par]], writes=[stB])
                    b.do("s", lambda h, aot=aot, par=par: h.activation(f32st[par], aot[:, 3 * TTW:4 * TTW], AF.Square),
                         reads=[aotB], writes=[f32stB[par]])
                    b.do("v", lambda h, st=st, par=par: h.reduce_sum(st[:, 4:5], f32st[par], axis=AX.X),
                         reads=[f32stB[par]], writes=[stB])
                    b.do("v", lambda h, st=st: h.tensor_add(st[:, 1:2], st[:, 1:2], st[:, 2:3]),
                         reads=[stB], writes=[stB])
                    b.do("v", lambda h, st=st: h.tensor_add(st[:, 3:4], st[:, 3:4], st[:, 4:5]),
                         reads=[stB], writes=[stB])
                    b.do("v", lambda h, st=st: h.tensor_add(st[:, 1:2], st[:, 1:2], st[:, 3:4]),
                         reads=[stB], writes=[stB])
                    _ln_apply(st, stB, aot, aotB, xt, xtB, t0, hbf, hbfB, hst, hstB,
                              psg, Pq)
            b.barrier()

        def _ln_apply(st2, st2B, aot, aotB, xt, xtB, t0, hbf, hbfB, hst, hstB, psg, Pq):
            b.do("v", lambda h: h.tensor_scalar(st2[:, 5:6], st2[:, 0:1], 1.0 / D, None, ALU.mult),
                 reads=[st2B], writes=[st2B])
            b.do("v", lambda h: h.tensor_mul(st2[:, 6:7], st2[:, 5:6], st2[:, 5:6]),
                 reads=[st2B], writes=[st2B])
            b.do("v", lambda h: h.scalar_tensor_tensor(st2[:, 7:8], st2[:, 1:2], 1.0 / D, st2[:, 6:7],
                                                       ALU.mult, ALU.subtract),
                 reads=[st2B], writes=[st2B])
            b.do("s", lambda h: h.activation(st2[:, 8:9], st2[:, 7:8], AF.Sqrt, bias=cbias[:, 0:1]),
                 reads=[st2B, B_("cbias")], writes=[st2B])
            b.do("v", lambda h: h.reciprocal(st2[:, 9:10], st2[:, 8:9]), reads=[st2B], writes=[st2B])
            b.do("v", lambda h: h.tensor_scalar(aot, aot, st2[:, 5:6], st2[:, 9:10],
                                                ALU.subtract, ALU.mult),
                 reads=[aotB, st2B], writes=[aotB])
            b.do("v", lambda h: h.tensor_add(xt, xt, aot), reads=[xtB, aotB], writes=[xtB])
            b.dma("y", x1_tok[t0:t0 + 128, :], xt, reads=[xtB])
            b.do("v", lambda h: h.reduce_sum(st2[:, 0:1], xt, axis=AX.X), reads=[xtB], writes=[st2B])
            b.do("s", lambda h: h.activation(aot, xt, AF.Square), reads=[xtB], writes=[aotB])
            b.do("v", lambda h: h.reduce_sum(st2[:, 1:2], aot, axis=AX.X), reads=[aotB], writes=[st2B])
            b.do("v", lambda h: h.tensor_scalar(st2[:, 5:6], st2[:, 0:1], 1.0 / D, None, ALU.mult),
                 reads=[st2B], writes=[st2B])
            b.do("v", lambda h: h.tensor_mul(st2[:, 6:7], st2[:, 5:6], st2[:, 5:6]),
                 reads=[st2B], writes=[st2B])
            b.do("v", lambda h: h.scalar_tensor_tensor(st2[:, 7:8], st2[:, 1:2], 1.0 / D, st2[:, 6:7],
                                                       ALU.mult, ALU.subtract),
                 reads=[st2B], writes=[st2B])
            b.do("s", lambda h: h.activation(st2[:, 8:9], st2[:, 7:8], AF.Sqrt, bias=cbias[:, 0:1]),
                 reads=[st2B, B_("cbias")], writes=[st2B])
            b.do("v", lambda h: h.reciprocal(st2[:, 9:10], st2[:, 8:9]), reads=[st2B], writes=[st2B])
            b.do("v", lambda h: h.tensor_scalar(xt, xt, st2[:, 5:6], st2[:, 9:10],
                                                ALU.subtract, ALU.mult),
                 reads=[xtB, st2B], writes=[xtB])
            for g in range(2):
                fns = [(lambda h, c=c: h.matmul(Pq[:, (c % 8) * 128:(c % 8) * 128 + 128],
                                                xt[:, 128 * c:128 * (c + 1)], identf,
                                                start=True, stop=True)) for c in range(8 * g, 8 * g + 8)]
                b.mmgroup(fns, reads=[xtB, B_("identf")], writes=psg)
                b.do("v", lambda h, g=g: h.tensor_copy(aot[:, 1024 * g:1024 * (g + 1)], Pq),
                     reads=psg, writes=[aotB])
            b.dma_group("y", [(f_inT[c, :, t0:t0 + 128], aot[:, 128 * c:128 * (c + 1)])
                              for c in range(NC)], reads=[aotB])
            b.do("s", lambda h: h.activation(hbf, xt, AF.Square), reads=[xtB], writes=[hbfB])
            b.do("v", lambda h: h.reduce_sum(st2[:, 10:11], hbf, axis=AX.X),
                 reads=[hbfB], writes=[st2B])
            b.do("v", lambda h: h.tensor_scalar(st2[:, 11:12], st2[:, 10:11], 1.0 / D, None, ALU.mult),
                 reads=[st2B], writes=[st2B])
            b.do("s", lambda h: h.activation(st2[:, 12:13], st2[:, 11:12], AF.Sqrt, bias=cbias[:, 1:2]),
                 reads=[st2B, B_("cbias")], writes=[st2B])
            b.do("v", lambda h: h.reciprocal(st2[:, 13:14], st2[:, 12:13]), reads=[st2B], writes=[st2B])
            b.do("v", lambda h: h.tensor_scalar(hbf, xt, st2[:, 13:14], None, ALU.mult),
                 reads=[xtB, st2B], writes=[hbfB])
            for g in range(2):
                fns = [(lambda h, c=c: h.matmul(Pq[:, (c % 8) * 128:(c % 8) * 128 + 128],
                                                hbf[:, 128 * c:128 * (c + 1)], ident,
                                                start=True, stop=True)) for c in range(8 * g, 8 * g + 8)]
                b.mmgroup(fns, reads=[hbfB, B_("ident")], writes=psg)
                for c in range(8 * g, 8 * g + 8):
                    eng = "v" if c % 2 == 0 else "s"
                    if eng == "v":
                        b.do("v", lambda h, c=c: h.tensor_scalar(
                                hst[:, 128 * c:128 * (c + 1)], Pq[:, (c % 8) * 128:(c % 8) * 128 + 128],
                                lnw1[:, c:c + 1], None, ALU.mult),
                             reads=psg + [B_("lnw1")], writes=[hstB])
                    else:
                        b.do("s", lambda h, c=c: h.activation(
                                hst[:, 128 * c:128 * (c + 1)], Pq[:, (c % 8) * 128:(c % 8) * 128 + 128],
                                AF.Identity, scale=lnw1[:, c:c + 1]),
                             reads=psg + [B_("lnw1")], writes=[hstB])
            b.dma_group("y", [(hT[c, :, t0:t0 + 128], hst[:, 128 * c:128 * (c + 1)])
                              for c in range(NC)], reads=[hstB])

        def phase_p6():
            ABF.reset()
            xb2 = [ABF.take([128, NC, TTW]) for _ in range(2)]
            xb2B = [B_("p6x%d" % i) for i in range(2)]
            h2b2 = [ABF.take([128, TTW]) for _ in range(2)]
            h2b2B = [B_("p6h%d" % i) for i in range(2)]
            sqb = ABF.take([128, TTW]); sqbB = B_("p6sq")
            ABF2.reset()
            wbuf = ABF2.take([128, NC, D]); wbufB = B_("p6w")
            AF32.reset()
            X1SB = AF32.take([128, NC, TTW])
            X1SBB = [B_("p6x1_%d" % i) for i in range(NC)]
            hf32 = AF32.take([128, TTW]); hf32B = B_("p6hf")
            rstdb = AF32.take([128, TTW]); rstdbB = B_("p6rstd")
            ftile = [AF32.take([128, TTW]) for _ in range(2)]
            ftileB = [B_("p6f%d" % i) for i in range(2)]
            AF32X.reset()
            strow = AF32X.take([1, TTW]); strowB = B_("p6st")
            rsm = AF32X.take([128, 40]); rsmB = B_("p6rsm")
            combst = AF32X.take([8, 128]); combstB = B_("p6comb")
            egrow = AF32X.take([1, TTW]); egrowB = B_("p6eg")

            wload(wbuf, wbufB, lambda k: wo_f[128 * k:128 * (k + 1), :])
            for tt in range(NTT):
                xb = xb2[tt % 2]; xbB = xb2B[tt % 2]
                b.dma_group("s", [(xb[:, k, :], ctxF[k, :, TTW * tt:TTW * (tt + 1)])
                                  for k in range(NC)], writes=[xbB])
                for m in range(NC):
                    bank = m % 4
                    ft = ftile[m % 2]; ftB = ftileB[m % 2]
                    b.dma("y", ft, f_inT[m, :, TTW * tt:TTW * (tt + 1)], writes=[ftB])
                    fns = [(lambda h, k=k, m=m, bank=bank, xb=xb:
                            h.matmul(PSAP[bank], wbuf[:, k, 128 * m:128 * (m + 1)], xb[:, k, :],
                                     start=(k == 0), stop=(k == NC - 1))) for k in range(NC)]
                    b.mmgroup(fns, reads=[wbufB, xbB], writes=[PSB[bank]])
                    b.do("v", lambda h, m=m, bank=bank, ft=ft:
                         h.tensor_add(X1SB[:, m, :], PSAP[bank], ft),
                         reads=[PSB[bank], ftB], writes=[X1SBB[m]])
                    b.dma("y", x1fT[m, :, TTW * tt:TTW * (tt + 1)], X1SB[:, m, :], reads=[X1SBB[m]])
                    b.do("s", lambda h, m=m: h.activation(sqb, X1SB[:, m, :], AF.Square),
                         reads=[X1SBB[m]], writes=[sqbB])
                    grp = b.mmgroup([(lambda h, m=m: h.matmul(P1[0:1, 0:TTW], onesb[:, 0:1], sqb,
                                                               start=(m == 0), stop=(m == NC - 1)))],
                                    reads=[sqbB, B_("onesb")],
                                    war=([PSB[4]] if m == 0 else ()),
                                    writes=([PSB[4]] if m == NC - 1 else ()))
                b.do("v", lambda h: h.tensor_copy(strow, P1[0:1, 0:TTW]), reads=[PSB[4]], writes=[strowB])
                b.do("v", lambda h: h.tensor_scalar(strow, strow, 1.0 / D, None, ALU.mult),
                     reads=[strowB], writes=[strowB])
                b.do("s", lambda h: h.activation(strow, strow, AF.Sqrt, bias=cbias[0:1, 1:2]),
                     reads=[strowB, B_("cbias")], writes=[strowB])
                b.do("v", lambda h: h.reciprocal(strow, strow), reads=[strowB], writes=[strowB])
                b.mmgroup([lambda h: h.matmul(P1[:, 1536:1536 + TTW], ones_row_f[0:1, :], strow,
                                              start=True, stop=True)],
                          reads=[strowB, B_("orf")], writes=[PSB[7]])
                b.do("v", lambda h: h.tensor_copy(rstdb, P1[:, 1536:1536 + TTW]),
                     reads=[PSB[7]], writes=[rstdbB])
                for m in range(NC):
                    hb = h2b2[m % 2]; hbB = h2b2B[m % 2]
                    b.do("v", lambda h, m=m: h.scalar_tensor_tensor(
                            hf32, X1SB[:, m, :], lnw2[:, m:m + 1], rstdb, ALU.mult, ALU.mult),
                         reads=[X1SBB[m], rstdbB, B_("lnw2")], writes=[hf32B])
                    b.do("s", lambda h, hb=hb: h.activation(hb, hf32, AF.Copy),
                         reads=[hf32B], writes=[hbB])
                    b.dma("y", h2T[m, :, TTW * tt:TTW * (tt + 1)], hb, reads=[hbB])
                    fns = [(lambda h, tb=tb, m=m: h.matmul(
                            PSAP[tb][:, 0:8],
                            hf32[:, 128 * tb:128 * (tb + 1)], wrt[:, m, :],
                            start=(m == 0), stop=(m == NC - 1))) for tb in range(4)]
                    b.mmgroup(fns, reads=[hf32B, B_("wrt")],
                              war=(PSB[0:4] if m == 0 else ()),
                              writes=(PSB[0:4] if m == NC - 1 else ()))
                    b.mmgroup([(lambda h, m=m, hb=hb: h.matmul(P1[0:1, 1024:1024 + TTW],
                                                               weg[:, m:m + 1], hb,
                                                               start=(m == 0), stop=(m == NC - 1)))],
                              reads=[hbB, B_("weg")],
                              war=([PSB[6]] if m == 0 else ()),
                              writes=([PSB[6]] if m == NC - 1 else ()))
                for tb in range(4):
                    b.do("v", lambda h, tb=tb: h.tensor_copy(rsm[:, 0:8], PSAP[tb][:, 0:8]),
                         reads=[PSB[tb]], writes=[rsmB])
                    b.do("v", lambda h: h.reduce_max(rsm[:, 8:9], rsm[:, 0:8], axis=AX.X, negate=True),
                         reads=[rsmB], writes=[rsmB])
                    b.do("s", lambda h: h.activation(rsm[:, 0:8], rsm[:, 0:8], AF.Exp,
                                                     bias=rsm[:, 8:9], accum_out=rsm[:, 9:10]),
                         reads=[rsmB], writes=[rsmB])
                    b.do("v", lambda h: h.reciprocal(rsm[:, 10:11], rsm[:, 9:10]),
                         reads=[rsmB], writes=[rsmB])
                    b.do("v", lambda h: h.tensor_scalar(rsm[:, 0:8], rsm[:, 0:8], rsm[:, 10:11],
                                                        None, ALU.mult),
                         reads=[rsmB], writes=[rsmB])
                    b.do("v", lambda h: h.reduce_max(rsm[:, 11:12], rsm[:, 0:8], axis=AX.X),
                         reads=[rsmB], writes=[rsmB])
                    b.do("v", lambda h: h.tensor_scalar(rsm[:, 16:24], rsm[:, 0:8], rsm[:, 11:12],
                                                        None, ALU.is_ge),
                         reads=[rsmB], writes=[rsmB])
                    b.do("v", lambda h: h.scalar_tensor_tensor(rsm[:, 16:24], rsm[:, 16:24], NEG_BIG,
                                                               rsm[:, 0:8], ALU.mult, ALU.add),
                         reads=[rsmB], writes=[rsmB])
                    b.do("v", lambda h: h.reduce_max(rsm[:, 12:13], rsm[:, 16:24], axis=AX.X),
                         reads=[rsmB], writes=[rsmB])
                    b.do("v", lambda h: h.tensor_scalar(rsm[:, 16:24], rsm[:, 0:8], rsm[:, 12:13],
                                                        None, ALU.is_ge),
                         reads=[rsmB], writes=[rsmB])
                    b.do("v", lambda h: h.tensor_mul(rsm[:, 24:32], rsm[:, 0:8], rsm[:, 16:24]),
                         reads=[rsmB], writes=[rsmB])
                    b.mmgroup([(lambda h: h.matmul(P1[0:8, 512:640], rsm[:, 24:32], identf,
                                                   start=True, stop=True))],
                              reads=[rsmB, B_("identf")], writes=[PSB[5]])
                    b.do("v", lambda h: h.tensor_copy(combst, P1[0:8, 512:640]),
                         reads=[PSB[5]], writes=[combstB])
                    t0 = TTW * tt + 128 * tb
                    b.dma("y", combT[:, t0:t0 + 128], combst, reads=[combstB])
                b.do("s", lambda h: h.activation(egrow, P1[0:1, 1024:1024 + TTW], AF.Sigmoid),
                     reads=[PSB[6]], writes=[egrowB])
                b.dma("y", sig_eg_d[0:1, TTW * tt:TTW * (tt + 1)], egrow, reads=[egrowB])
            b.barrier()

        def phase_p7a():
            MW = 512
            nmt = T // MW
            ABF.reset()
            wu = ABF.take([128, NC, MOE_I]); wuB = B_("p7wu")
            xb_a = ABF.take([128, NC, MW])
            midb2 = [ABF.take([128, MW])]
            midb2B = [B_("p7m0")]
            ABF2.reset()
            wg = ABF2.take([128, NC, MOE_I]); wgB = B_("p7wg")
            xb_b = ABF2.take([128, NC, MW])
            wcst = [ABF2.take([128, D])]
            wcstB = [B_("p7wc0")]
            xb2 = [xb_a, xb_b]
            xb2B = [B_("p7x%d" % i) for i in range(2)]
            AF32.reset()
            sg = AF32.take([128, MW]); sgB = B_("p7sg")
            combb = AF32.take([128, MW]); combbB = B_("p7cb")
            crow = AF32.take([1, T]); crowB = B_("p7crow")

            def precast(lo, hi):
                for kidx in range(lo, min(hi, NE * NMC + NSC)):
                    wc = wcst[0]; wcB = wcstB[0]
                    if kidx < NE * NMC:
                        ee, mc = kidx // NMC, kidx % NMC
                        csrc = w_down_e[ee, 128 * mc:128 * (mc + 1), :]
                    else:
                        sc = kidx - NE * NMC
                        csrc = w_sh_down[128 * sc:128 * (sc + 1), :]
                    b.dma("g", wc, csrc, writes=[wcB])
                    b.dma("y", wdn_bf[kidx, :, :], wc, reads=[wcB])

            for e in range(NE):
                precast(17 * e, 17 * (e + 1) if e < NE - 1 else NE * NMC + NSC)
                b.dma_group("g", [(wg[:, k, :], w_gate_e[e, 128 * k:128 * (k + 1), :])
                                  for k in range(NC)], writes=[wgB])
                b.dma_group("g", [(wu[:, k, :], w_up_e[e, 128 * k:128 * (k + 1), :])
                                  for k in range(NC)], writes=[wuB])
                b.dma("s", crow, combT[e:e + 1, :], writes=[crowB])
                for ttt in range(nmt):
                    xb = xb2[ttt % 2]; xbB = xb2B[ttt % 2]
                    b.dma_group("s", [(xb[:, k, :], h2T[k, :, MW * ttt:MW * (ttt + 1)])
                                      for k in range(NC)], writes=[xbB])
                    b.mmgroup([(lambda h, ttt=ttt: h.matmul(P1[:, 1536:1536 + MW], ones_row_f[0:1, :],
                                                            crow[0:1, MW * ttt:MW * (ttt + 1)],
                                                            start=True, stop=True))],
                              reads=[crowB, B_("orf")], writes=[PSB[7]])
                    b.do("v", lambda h: h.tensor_copy(combb, P1[:, 1536:1536 + MW]),
                         reads=[PSB[7]], writes=[combbB])
                    for mc in range(NMC):
                        bk_g = mc % 2
                        bk_u = 2 + mc % 2
                        fns = [(lambda h, k=k, mc=mc, bk_g=bk_g, xb=xb:
                                h.matmul(PSAP[bk_g][:, 0:MW], wg[:, k, 128 * mc:128 * (mc + 1)],
                                         xb[:, k, :], start=(k == 0), stop=(k == NC - 1)))
                               for k in range(NC)]
                        b.mmgroup(fns, reads=[wgB, xbB], writes=[PSB[bk_g]])
                        fns = [(lambda h, k=k, mc=mc, bk_u=bk_u, xb=xb:
                                h.matmul(PSAP[bk_u][:, 0:MW], wu[:, k, 128 * mc:128 * (mc + 1)],
                                         xb[:, k, :], start=(k == 0), stop=(k == NC - 1)))
                               for k in range(NC)]
                        b.mmgroup(fns, reads=[wuB, xbB], writes=[PSB[bk_u]])
                        mb = midb2[0]; mbB = midb2B[0]
                        b.do("s", lambda h, bk_g=bk_g: h.activation(sg, PSAP[bk_g][:, 0:MW], AF.Silu),
                             reads=[PSB[bk_g]], writes=[sgB])
                        b.do("v", lambda h: h.tensor_mul(sg, sg, combb),
                             reads=[sgB, combbB], writes=[sgB])
                        b.do("v", lambda h, bk_u=bk_u, mb=mb: h.tensor_tensor(
                                mb, sg, PSAP[bk_u][:, 0:MW], ALU.mult),
                             reads=[sgB, PSB[bk_u]], writes=[mbB])
                        b.dma("y", midT[e, mc, :, MW * ttt:MW * (ttt + 1)], mb, reads=[mbB])
            b.barrier()

        def phase_p8a():
            NSTR = SH_I // TTW
            ABF.reset()
            xb2 = [ABF.take([128, NC, TTW]) for _ in range(2)]
            xb2B = [B_("p8x%d" % i) for i in range(2)]
            smb2 = [ABF.take([128, TTW]) for _ in range(2)]
            smb2B = [B_("p8m%d" % i) for i in range(2)]
            ABF2.reset()
            wg2 = [ABF2.take([128, NC, TTW]) for _ in range(2)]
            wg2B = [B_("p8wg%d" % i) for i in range(2)]
            wu2 = [ABF2.take([128, NC, TTW]) for _ in range(2)]
            wu2B = [B_("p8wu%d" % i) for i in range(2)]
            AF32.reset()
            sg = AF32.take([128, TTW]); sgB = B_("p8sg")
            srow = AF32.take([1, T]); srowB = B_("p8srow")
            AF32X.reset()
            sigb = AF32X.take([128, NTT, TTW]); sigbB = B_("p8sigb")

            b.dma("s", srow, sig_eg_d[0:1, :], writes=[srowB])
            for tt in range(NTT):
                b.mmgroup([(lambda h, tt=tt: h.matmul(P1[:, 1536:1536 + TTW], ones_row_f[0:1, :],
                                                      srow[0:1, TTW * tt:TTW * (tt + 1)],
                                                      start=True, stop=True))],
                          reads=[srowB, B_("orf")], writes=[PSB[7]])
                b.do("v", lambda h, tt=tt: h.tensor_copy(sigb[:, tt, :], P1[:, 1536:1536 + TTW]),
                     reads=[PSB[7]], writes=[sigbB])

            for si in range(NSTR):
                par = si % 2
                wgs = wg2[par]; wgsB = wg2B[par]
                wus = wu2[par]; wusB = wu2B[par]
                b.dma_group("g", [(wgs[:, k, :], w_sh_gate[128 * k:128 * (k + 1),
                                                           TTW * si:TTW * (si + 1)])
                                  for k in range(NC)], writes=[wgsB])
                b.dma_group("g", [(wus[:, k, :], w_sh_up[128 * k:128 * (k + 1),
                                                         TTW * si:TTW * (si + 1)])
                                  for k in range(NC)], writes=[wusB])
                for tt in range(NTT):
                    xb = xb2[tt % 2]; xbB = xb2B[tt % 2]
                    b.dma_group("s", [(xb[:, k, :], h2T[k, :, TTW * tt:TTW * (tt + 1)])
                                      for k in range(NC)], writes=[xbB])
                    for ms in range(4):
                        bk_g = ms % 2
                        bk_u = 2 + ms % 2
                        fns = [(lambda h, k=k, ms=ms, bk_g=bk_g, xb=xb, wgs=wgs:
                                h.matmul(PSAP[bk_g], wgs[:, k, 128 * ms:128 * (ms + 1)],
                                         xb[:, k, :], start=(k == 0), stop=(k == NC - 1)))
                               for k in range(NC)]
                        b.mmgroup(fns, reads=[wgsB, xbB], writes=[PSB[bk_g]])
                        fns = [(lambda h, k=k, ms=ms, bk_u=bk_u, xb=xb, wus=wus:
                                h.matmul(PSAP[bk_u], wus[:, k, 128 * ms:128 * (ms + 1)],
                                         xb[:, k, :], start=(k == 0), stop=(k == NC - 1)))
                               for k in range(NC)]
                        b.mmgroup(fns, reads=[wusB, xbB], writes=[PSB[bk_u]])
                        mb = smb2[ms % 2]; mbB = smb2B[ms % 2]
                        b.do("s", lambda h, bk_g=bk_g: h.activation(sg, PSAP[bk_g], AF.Silu),
                             reads=[PSB[bk_g]], writes=[sgB])
                        b.do("v", lambda h, tt=tt: h.tensor_mul(sg, sg, sigb[:, tt, :]),
                             reads=[sgB, sigbB], writes=[sgB])
                        b.do("v", lambda h, bk_u=bk_u, mb=mb: h.tensor_tensor(
                                mb, sg, PSAP[bk_u], ALU.mult),
                             reads=[sgB, PSB[bk_u]], writes=[mbB])
                        b.dma("y", smidT[4 * si + ms, :, TTW * tt:TTW * (tt + 1)], mb, reads=[mbB])
            b.barrier()

        def phase_p8b():
            NK = NE * NMC + NSC
            ABF.reset()
            rh4 = [ABF.take([128, TTW]) for _ in range(4)]
            rh4B = [B_("pbr%d" % i) for i in range(4)]
            ws4 = [ABF.take([128, 1024]) for _ in range(4)]
            ws4B = [B_("pbw%d" % i) for i in range(4)]
            AF32.reset()
            xf2 = [AF32.take([128, TTW]) for _ in range(2)]
            xf2B = [B_("pbx%d" % i) for i in range(2)]
            ys2 = [AF32.take([128, TTW]) for _ in range(2)]
            ys2B = [B_("pby%d" % i) for i in range(2)]

            def ksrc(kidx, tt):
                if kidx < NE * NMC:
                    e, mc = kidx // NMC, kidx % NMC
                    return midT[e, mc, :, TTW * tt:TTW * (tt + 1)]
                return smidT[kidx - NE * NMC, :, TTW * tt:TTW * (tt + 1)]

            for mh in range(2):
                for tt in range(NTT):
                    for kidx in range(NK):
                        rh = rh4[kidx % 4]; rhB = rh4B[kidx % 4]
                        ws = ws4[kidx % 4]; wsB = ws4B[kidx % 4]
                        b.dma("s", rh, ksrc(kidx, tt), writes=[rhB])
                        b.dma("g", ws, wdn_bf[kidx, :, 1024 * mh:1024 * (mh + 1)], writes=[wsB])
                        fns = [(lambda h, m=m, rh=rh, ws=ws, kidx=kidx:
                                h.matmul(PSAP[m], ws[:, 128 * m:128 * (m + 1)], rh,
                                         start=(kidx == 0), stop=(kidx == NK - 1)))
                               for m in range(8)]
                        b.mmgroup(fns, reads=[rhB, wsB],
                                  war=(PSB if kidx == 0 else ()),
                                  writes=(PSB if kidx == NK - 1 else ()))
                    for m in range(8):
                        xf = xf2[m % 2]; xfB = xf2B[m % 2]
                        ys = ys2[m % 2]; ysB = ys2B[m % 2]
                        b.dma("y", xf, x1fT[8 * mh + m, :, TTW * tt:TTW * (tt + 1)], writes=[xfB])
                        b.do("v", lambda h, m=m, xf=xf, ys=ys: h.tensor_add(ys, PSAP[m], xf),
                             reads=[PSB[m], xfB], writes=[ysB])
                        b.dma("y", y2T[8 * mh + m, :, TTW * tt:TTW * (tt + 1)], ys, reads=[ysB])
            b.barrier()

        def phase_final():
            AF32.reset()
            y2c2 = [AF32.take([128, NC, 128]) for _ in range(2)]
            y2c2B = [B_("p9c%d" % i) for i in range(2)]
            y2tok2 = [AF32.take([128, D]) for _ in range(2)]
            y2tok2B = [B_("p9t%d" % i) for i in range(2)]
            x1t2 = [AF32.take([128, D]) for _ in range(2)]
            x1t2B = [B_("p9x%d" % i) for i in range(2)]
            AF32X.reset()
            st4 = AF32X.take([128, 32]); st4B = [B_("p9st0"), B_("p9st1")]
            sq22 = [AF32X.take([128, D]) for _ in range(2)]
            sq22B = [B_("p9sq%d" % i) for i in range(2)]
            outsb = AF32X.take([128, 1024]); outsbB = B_("p9o")
            for tb in range(T // 128):
                par = tb % 2
                y2c, y2cB = y2c2[par], y2c2B[par]
                y2tok, y2tokB = y2tok2[par], y2tok2B[par]
                x1t, x1tB = x1t2[par], x1t2B[par]
                sq2, sq2B = sq22[par], sq22B[par]
                st2 = st4[:, 16 * par:16 * par + 16]; st2B = st4B[par]
                Pq = P1[:, 0:1024] if par == 0 else P1[:, 1024:2048]
                psg = PSB[4:6] if par == 0 else PSB[6:8]
                b.dma_group("y", [(y2c[:, c, :], y2T[c, :, 128 * tb:128 * (tb + 1)])
                                  for c in range(NC)], writes=[y2cB])
                b.dma("s", x1t, x1_tok[128 * tb:128 * (tb + 1), :], writes=[x1tB])
                for g in range(2):
                    fns = [(lambda h, c=c, Pq=Pq, y2c=y2c: h.matmul(
                            Pq[:, (c % 8) * 128:(c % 8) * 128 + 128],
                            y2c[:, c, :], identf, start=True, stop=True))
                           for c in range(8 * g, 8 * g + 8)]
                    b.mmgroup(fns, reads=[y2cB, B_("identf")], writes=psg)
                    b.do("v", lambda h, g=g, y2tok=y2tok, Pq=Pq: h.tensor_copy(
                            y2tok[:, 1024 * g:1024 * (g + 1)], Pq),
                         reads=psg, writes=[y2tokB])
                b.do("v", lambda h, y2tok=y2tok, st2=st2: h.reduce_sum(st2[:, 0:1], y2tok, axis=AX.X),
                     reads=[y2tokB], writes=[st2B])
                b.do("s", lambda h, y2tok=y2tok, sq2=sq2: h.activation(sq2, y2tok, AF.Square),
                     reads=[y2tokB], writes=[sq2B])
                b.do("v", lambda h, sq2=sq2, st2=st2: h.reduce_sum(st2[:, 1:2], sq2, axis=AX.X),
                     reads=[sq2B], writes=[st2B])
                b.do("v", lambda h, st2=st2: h.tensor_scalar(st2[:, 2:3], st2[:, 0:1], 1.0 / D, None, ALU.mult),
                     reads=[st2B], writes=[st2B])
                b.do("v", lambda h, st2=st2: h.tensor_mul(st2[:, 3:4], st2[:, 2:3], st2[:, 2:3]),
                     reads=[st2B], writes=[st2B])
                b.do("v", lambda h, st2=st2: h.scalar_tensor_tensor(st2[:, 4:5], st2[:, 1:2], 1.0 / D,
                                                                    st2[:, 3:4], ALU.mult, ALU.subtract),
                     reads=[st2B], writes=[st2B])
                b.do("s", lambda h, st2=st2: h.activation(st2[:, 5:6], st2[:, 4:5], AF.Sqrt, bias=cbias[:, 0:1]),
                     reads=[st2B, B_("cbias")], writes=[st2B])
                b.do("v", lambda h, st2=st2: h.reciprocal(st2[:, 6:7], st2[:, 5:6]), reads=[st2B], writes=[st2B])
                b.do("v", lambda h, y2tok=y2tok, st2=st2: h.tensor_scalar(
                        y2tok, y2tok, st2[:, 2:3], st2[:, 6:7], ALU.subtract, ALU.mult),
                     reads=[y2tokB, st2B], writes=[y2tokB])
                b.do("v", lambda h, y2tok=y2tok, x1t=x1t: h.tensor_add(y2tok, y2tok, x1t),
                     reads=[y2tokB, x1tB], writes=[y2tokB])
                b.dma("y", out[128 * tb:128 * (tb + 1), :], y2tok, reads=[y2tokB])
            b.barrier()

        setup_phase()
        b.barrier()
        order = [("ln0", phase_ln0),
                 ("qkvA", lambda: phase_qkv("A")),
                 ("attnA", lambda: phase_attn("A")),
                 ("p3", phase_p3),
                 ("qkvF", lambda: phase_qkv("F")),
                 ("attnF", lambda: phase_attn("F")),
                 ("p6", phase_p6),
                 ("p7a", phase_p7a),
                 ("p8a", phase_p8a),
                 ("p8b", phase_p8b),
                 ("final", phase_final)]
        for name, fn in order:
            fn()
            if stop_after == name:
                break

        b.barrier()
        _play(nc, b)
    return nc


def _play(nc, b):
    with nc.Block() as block:
        @block.tensor
        def _(h):
            for f in b.ops["t"]:
                f(h)

        @block.vector
        def _(h):
            for f in b.ops["v"]:
                f(h)

        @block.scalar
        def _(h):
            for f in b.ops["s"]:
                f(h)

        @block.gpsimd
        def _(h):
            for f in b.ops["g"]:
                f(h)

        @block.sync
        def _(h):
            for f in b.ops["y"]:
                f(h)



def _prep_inputs(inp):
    f32 = lambda a: np.ascontiguousarray(np.asarray(a, dtype=np.float32))
    g = {}
    g["attn_Wq"] = f32(inp["L0_attn_Wq"]); g["attn_Wk"] = f32(inp["L0_attn_Wk"])
    g["attn_Wv"] = f32(inp["L0_attn_Wv"]); g["attn_Wo"] = f32(inp["L0_attn_Wo"])
    g["splat_pos"] = f32(inp["L0_attn_splat_positions"]).reshape(128, 128)
    g["splat_dir"] = f32(inp["L0_attn_splat_directions"]).reshape(128, 128)
    g["splat_lsc"] = f32(inp["L0_attn_splat_log_scales"]).reshape(128, 1)
    g["splat_lam"] = f32(inp["L0_attn_splat_log_amplitudes"]).reshape(16, 8)
    g["splat_ds"] = f32(inp["L0_attn_directional_strength"]).reshape(1, 1)
    g["splat_gsa"] = f32(inp["L0_attn_gsa_strength"]).reshape(1, 1)
    g["w_ln1"] = _fm_vec(inp["L0_ffn_w_ln1"]); g["w_ln2"] = _fm_vec(inp["L0_ffn_w_ln2"])
    g["w_ln1_row"] = f32(inp["L0_ffn_w_ln1"]).reshape(1, D)
    g["ffn_bq"] = _fm_vec(inp["L0_ffn_bq"]); g["ffn_bk"] = _fm_vec(inp["L0_ffn_bk"])
    g["ffn_bv"] = f32(inp["L0_ffn_bv"]).reshape(1, D)
    g["ffn_wq"] = f32(inp["L0_ffn_wq"]); g["ffn_wk"] = f32(inp["L0_ffn_wk"])
    g["ffn_wv"] = f32(inp["L0_ffn_wv"]); g["ffn_wo"] = f32(inp["L0_ffn_wo"])
    g["w_router"] = f32(inp["L0_ffn_w_router"])
    g["w_gate_e"] = f32(inp["L0_ffn_w_gate_e"])
    g["w_up_e"] = f32(inp["L0_ffn_w_up_e"])
    g["w_down_e"] = f32(inp["L0_ffn_w_down_e"])
    g["w_sh_gate"] = f32(inp["L0_ffn_w_sh_gate"])
    g["w_sh_up"] = f32(inp["L0_ffn_w_sh_up"])
    g["w_sh_down"] = f32(inp["L0_ffn_w_sh_down"])
    g["w_sh_eg"] = _fm_vec(inp["L0_ffn_w_sh_eg"])
    g.update(_host_constants())
    return g


def kernel(**inp):
    nc = build_program()
    shared = _prep_inputs(inp)
    x = np.ascontiguousarray(np.asarray(inp["L0_attn_hidden_states"], dtype=np.float32))
    in_maps = []
    for c in range(NCORES):
        m = dict(shared)
        m["x"] = np.ascontiguousarray(x[SEQ_PC * c:SEQ_PC * (c + 1)].reshape(T, D))
        in_maps.append(m)
    res = run_bass_kernel_spmd(nc, in_maps, core_ids=list(range(NCORES)))
    parts = [res.results[c]["out"].reshape(SEQ_PC, S, D) for c in range(NCORES)]
    return np.ascontiguousarray(np.concatenate(parts, axis=0), dtype=np.float32)
